# version 10
# baseline (speedup 1.0000x reference)
import numpy as np
import concourse.bass as bass
import concourse.mybir as mybir
from concourse.bass_utils import run_bass_kernel_spmd

F32 = mybir.dt.float32
BF16 = mybir.dt.bfloat16
AF = mybir.ActivationFunctionType
ALU = mybir.AluOpType
AX = mybir.AxisListType

T = 1024
TC = 8
D = 2048
NDC = 16
NHC = 64
NW = 4
HEADS = 16
LN_EPS = 1e-5
RMS_EPS = 1.1920929e-07
ATT_EPS = 1e-6
PHASES = 9


def build():
    nc = bass.Bass("TRN2", target_bir_lowering=False)

    x_in = nc.dram_tensor("x_slice", [T, D], F32, kind="ExternalInput").ap()
    attw = nc.dram_tensor("attw_slice", [1024, D], F32, kind="ExternalInput").ap()
    wgu_in = nc.dram_tensor("wgu_slice", [2048, D], F32, kind="ExternalInput").ap()
    wpj_in = nc.dram_tensor("wproj_slice", [2048, 1024], F32,
                            kind="ExternalInput").ap()
    qb_col = nc.dram_tensor("qb_col", [128, 16], F32, kind="ExternalInput").ap()
    ident_in = nc.dram_tensor("ident_in", [128, 128], F32, kind="ExternalInput").ap()
    kb_row = nc.dram_tensor("kb_row", [1, D], F32, kind="ExternalInput").ap()
    vb_row = nc.dram_tensor("vb_row", [1, D], F32, kind="ExternalInput").ap()
    ob_row = nc.dram_tensor("ob_row", [1, D], F32, kind="ExternalInput").ap()
    out_d = nc.dram_tensor("out_slice", [T, D], F32, kind="ExternalOutput").ap()

    xn_nat = nc.dram_tensor("xn_nat", [T, D], BF16)
    attw_nat = nc.dram_tensor("attw_nat", [1024, D], BF16)
    sh_att = nc.dram_tensor("sh_att", [128, 16 * 1024], BF16)
    sh_wgu = nc.dram_tensor("sh_wgu", [128, 16 * 2048], BF16)
    sh_wpj = nc.dram_tensor("sh_wpj", [128, 8 * 2048], BF16)
    g_att = nc.dram_tensor("g_att", [1024, 16 * 1024], BF16, addr_space="Shared")
    g_wgu = nc.dram_tensor("g_wgu", [1024, 16 * 2048], BF16, addr_space="Shared")
    g_wpj = nc.dram_tensor("g_wpj", [1024, 8 * 2048], BF16, addr_space="Shared")
    ar_i = nc.dram_tensor("ar_i", [128, HEADS * 132], F32)
    ar_o = nc.dram_tensor("ar_o", [128, HEADS * 132], F32)
    part_nat = nc.dram_tensor("part_nat", [T, D], F32)

    gv_att = g_att.ap().rearrange("(r p) (c f) -> r p c f", p=128, f=1024)
    gv_wgu = g_wgu.ap().rearrange("(r p) (c f) -> r p c f", p=128, f=2048)
    gv_wpj = g_wpj.ap().rearrange("(r p) (c f) -> r p c f", p=128, f=2048)

    from contextlib import ExitStack
    with ExitStack() as st0:
        def sb(name, shape, dt, stk=None):
            return (stk or st0).enter_context(nc.sbuf_tensor(name, shape, dt))

        def pst(name, shape, stk=None, dt=F32):
            return (stk or st0).enter_context(nc.psum_tensor(name, shape, dt))

        sems = {}

        def S(name):
            if name not in sems:
                sems[name] = st0.enter_context(nc.semaphore(name))
            return sems[name]

        xn_fm = sb("xn_fm", [128, NDC, T], BF16)
        eps_ln = sb("eps_ln", [128, 1], F32)
        eps_rms = sb("eps_rms", [128, 1], F32)
        rstd_f = sb("rstd_f", [128, 8], F32)
        stB = st0.enter_context(ExitStack())
        kv_b = sb("kv_b", [128, HEADS * 132], BF16, stB)
        kv_f = sb("kv_f", [128, HEADS * 132], F32, stB)
        qbc = sb("qbc", [128, 16], F32, stB)
        kbr_h = sb("kbr_h", [1, D], BF16, stB)
        kbr_l = sb("kbr_l", [1, D], BF16, stB)
        vbr_h = sb("vbr_h", [1, D], BF16, stB)
        vbr_l = sb("vbr_l", [1, D], BF16, stB)
        obr_h = sb("obr_h", [1, D], BF16, stB)
        obr_l = sb("obr_l", [1, D], BF16, stB)
        ones_r = sb("ones_r", [1, 512], BF16, stB)
        identb = sb("identb", [128, 128], BF16, stB)


        with ExitStack() as stP0:
            kbr = sb("kbr", [1, D], F32, stP0)
            vbr = sb("vbr", [1, D], F32, stP0)
            obr = sb("obr", [1, D], F32, stP0)
            brow_f = sb("brow_f", [1, D], F32, stP0)
            xb = [sb("xb0", [128, D], F32, stP0),
                  sb("xb1", [128, D], F32, stP0)]
            xb2 = [sb("xb20", [128, D], F32, stP0),
                   sb("xb21", [128, D], F32, stP0)]
            xnb = [sb("xnb0", [128, D], BF16, stP0),
                   sb("xnb1", [128, D], BF16, stP0)]
            sq_d = sb("sq_d", [128, D], BF16, stP0)
            sums = sb("sums", [128, 8], F32, stP0)
            sumsq = sb("sumsq", [128, 8], F32, stP0)
            mean8 = sb("mean8", [128, 8], F32, stP0)
            var8 = sb("var8", [128, 8], F32, stP0)
            rstd8 = sb("rstd8", [128, 8], F32, stP0)
            nmr8 = sb("nmr8", [128, 8], F32, stP0)

            with nc.Block() as blk:
                @blk.sync
                def _(sync):
                    sync.dma_start(out=qbc[:, :], in_=qb_col).then_inc(S("dLD"), 16)
                    sync.dma_start(out=kbr[:, :], in_=kb_row).then_inc(S("dLD"), 16)
                    sync.dma_start(out=vbr[:, :], in_=vb_row).then_inc(S("dLD"), 16)
                    sync.dma_start(out=obr[:, :], in_=ob_row).then_inc(S("dLD"), 16)
                    for i in range(TC):
                        if i >= 2:
                            sync.wait_ge(S("vSUM"), i - 1)
                            sync.wait_ge(S("aSQ"), i - 1)
                        sync.dma_start(
                            out=xb[i % 2][:, :],
                            in_=x_in[i * 128:(i + 1) * 128, :]
                        ).then_inc(S("dX1%d" % (i % 2)), 16)
                    for i in range(TC):
                        if i >= 2:
                            sync.wait_ge(S("aXN"), i - 1)
                        sync.dma_start(
                            out=xb2[i % 2][:, :],
                            in_=x_in[i * 128:(i + 1) * 128, :]
                        ).then_inc(S("dX2%d" % (i % 2)), 16)
                        if i >= 1:
                            sync.wait_ge(S("aXN"), i)
                            sync.dma_start(
                                out=xn_nat.ap()[(i - 1) * 128:i * 128, :],
                                in_=xnb[(i - 1) % 2][:, :]
                            ).then_inc(S("dXNs%d" % ((i - 1) % 2)), 16)
                    sync.wait_ge(S("aXN"), TC)
                    sync.dma_start(
                        out=xn_nat.ap()[(TC - 1) * 128:TC * 128, :],
                        in_=xnb[(TC - 1) % 2][:, :]
                    ).then_inc(S("dXNs%d" % ((TC - 1) % 2)), 16)
                    sync.wait_ge(S("dXNs0"), 64)
                    sync.wait_ge(S("dXNs1"), 64)
                    sync.wait_ge(S("dAW"), 16)

                @blk.gpsimd
                def _(g):
                    g.memset(ones_r[:, :], 1.0)
                    g.memset(eps_ln[:, :], LN_EPS)
                    g.memset(eps_rms[:, :], RMS_EPS)
                    g.dma_start(out=attw_nat.ap(), in_=attw
                                ).then_inc(S("dAW"), 16)
                    g.dma_start(out=kbr_h[:, :], in_=kb_row).then_inc(S("dCAST"), 16)
                    g.dma_start(out=vbr_h[:, :], in_=vb_row).then_inc(S("dCAST"), 16)
                    g.dma_start(out=obr_h[:, :], in_=ob_row).then_inc(S("dCAST"), 16)
                    g.dma_start(out=identb[:, :], in_=ident_in).then_inc(S("dCAST"), 16)

                @blk.vector
                def _(v):
                    for tc in range(TC):
                        v.wait_ge(S("dX1%d" % (tc % 2)), 16 * (tc // 2 + 1))
                        v.tensor_reduce(sums[:, tc:tc + 1], xb[tc % 2][:, :],
                                        axis=AX.X, op=ALU.add
                                        ).then_inc(S("vSUM"), 1)
                    v.wait_ge(S("dLD"), 64)
                    v.wait_ge(S("dCAST"), 64)
                    nbl = 0
                    for b_f, b_h, b_l in ((kbr, kbr_h, kbr_l),
                                          (vbr, vbr_h, vbr_l),
                                          (obr, obr_h, obr_l)):
                        v.tensor_tensor(brow_f[:, :], b_f[:, :], b_h[:, :],
                                        op=ALU.subtract).then_inc(S("vBL"), 1)
                        nbl += 1
                        v.wait_ge(S("vBL"), nbl)
                        v.tensor_copy(b_l[:, :], brow_f[:, :]).then_inc(S("vBL"), 1)
                        nbl += 1
                        v.wait_ge(S("vBL"), nbl)
                    v.wait_ge(S("vSUM"), 8)
                    v.wait_ge(S("aSQ"), 8)
                    n = [0]

                    def stp(instr):
                        instr.then_inc(S("vST"), 1)
                        n[0] += 1
                        v.wait_ge(S("vST"), n[0])
                    stp(v.tensor_scalar(mean8[:, :], sums[:, :], 1.0 / D, None,
                                        op0=ALU.mult))
                    stp(v.tensor_tensor(var8[:, :], mean8[:, :], mean8[:, :],
                                        op=ALU.mult))
                    stp(v.tensor_scalar(sumsq[:, :], sumsq[:, :], 1.0 / D,
                                        None, op0=ALU.mult))
                    stp(v.tensor_tensor(var8[:, :], sumsq[:, :], var8[:, :],
                                        op=ALU.subtract))
                    v.wait_ge(S("aRSD"), 1)
                    stp(v.reciprocal(rstd8[:, :], var8[:, :]))
                    stp(v.tensor_tensor(nmr8[:, :], mean8[:, :], rstd8[:, :],
                                        op=ALU.mult))
                    stp(v.tensor_scalar(nmr8[:, :], nmr8[:, :], -1.0, None,
                                        op0=ALU.mult))

                @blk.scalar
                def _(s):
                    for tc in range(TC):
                        s.wait_ge(S("dX1%d" % (tc % 2)), 16 * (tc // 2 + 1))
                        if tc >= 1:
                            s.wait_ge(S("aSQ"), tc)
                        s.activation(sq_d[:, :], xb[tc % 2][:, :], AF.Square,
                                     accum_out=sumsq[:, tc:tc + 1]
                                     ).then_inc(S("aSQ"), 1)
                    s.wait_ge(S("vST"), 4)
                    s.activation(var8[:, :], var8[:, :], AF.Sqrt,
                                 bias=eps_ln[:, :]).then_inc(S("aRSD"), 1)
                    s.wait_ge(S("vST"), 7)
                    for tc in range(TC):
                        s.wait_ge(S("dX2%d" % (tc % 2)), 16 * (tc // 2 + 1))
                        if tc >= 2:
                            s.wait_ge(S("dXNs%d" % (tc % 2)), 16 * (tc // 2))
                        s.activation(xnb[tc % 2][:, :], xb2[tc % 2][:, :],
                                     AF.Identity, bias=nmr8[:, tc:tc + 1],
                                     scale=rstd8[:, tc:tc + 1]
                                     ).then_inc(S("aXN"), 1)

        with ExitStack() as stP0b:
            tr_att = sb("tr_att", [128, 16, 1024], BF16, stP0b)

            with nc.Block() as blk:
                @blk.sync
                def _(sync):
                    sync.dma_start_transpose(out=xn_fm[:, :, :],
                                             in_=xn_nat.ap()
                                             ).then_inc(S("dTR"), 16)
                    sync.wait_ge(S("dTR"), 16)
                    sync.dma_start_transpose(out=tr_att[:, :, :],
                                             in_=attw_nat.ap()
                                             ).then_inc(S("dTR"), 16)
                    sync.wait_ge(S("dTR"), 32)
                    sync.dma_start(
                        out=sh_att.ap().rearrange("p (c f) -> p c f", f=1024),
                        in_=tr_att[:, :, :]).then_inc(S("dGA"), 16)
                    sync.wait_ge(S("dGA"), 16)

                @blk.gpsimd
                def _(g):
                    g.wait_ge(S("dGA"), 16)
                    g.collective_compute(
                        "AllGather", ALU.bypass,
                        replica_groups=[list(range(8))],
                        ins=[sh_att.ap().opt()], outs=[g_att.ap().opt()]
                    ).then_inc(S("sAF"), 1)

        with ExitStack() as stP0c:
            wb = [sb("wb0", [128, D], BF16, stP0c),
                  sb("wb1", [128, D], BF16, stP0c)]
            qt_gr = [sb("qt_gr%d" % k, [128, 32], F32, stP0c) for k in range(2)]
            qt_s = [sb("qt_s%d" % k, [128, 32], BF16, stP0c) for k in range(2)]
            qt_t = [sb("qt_t%d" % k, [128, 32], F32, stP0c) for k in range(2)]
            qt_nt = [sb("qt_nt%d" % k, [128, 32], F32, stP0c) for k in range(2)]
            qt_c = [sb("qt_c%d" % k, [128, D], BF16, stP0c) for k in range(2)]
            qt_u = [sb("qt_u%d" % k, [128, D], BF16, stP0c) for k in range(2)]
            qt_q = [sb("qt_q%d" % k, [128, D], BF16, stP0c) for k in range(2)]
            qt_w = [sb("qt_w%d" % k, [128, D], BF16, stP0c) for k in range(2)]
            qt_r = [sb("qt_r%d" % k, [128, D], BF16, stP0c) for k in range(2)]
            stg = [sb("stg%d" % k, [128, 16, 512], BF16, stP0c) for k in range(2)]
            wf = [sb("wf%d" % k, [128, D], F32, stP0c) for k in range(2)]
            ptp = [pst("ptp%d" % k, [128, 512], stP0c, dt=BF16) for k in range(2)]

            NT = 24

            def vq_after_B(i):
                return 6 * min(i + 2, NT) + 3 * (i + 1)

            def tile_A(v, i):
                k = i % 2
                buf = wb[k]
                v.wait_ge(S("aWB"), i + 1)
                def stq(instr):
                    instr.then_inc(S("vQ"), 1)
                    tile_A.n += 1
                    v.wait_ge(S("vQ"), tile_A.n)
                stq(v.tensor_reduce(
                    qt_gr[k][:, :],
                    buf[:, :].rearrange("p (g c) -> p g c", c=64),
                    axis=AX.X, op=ALU.add, apply_absolute_value=True))
                stq(v.tensor_scalar(qt_s[k][:, :], qt_gr[k][:, :], 1.0 / 64,
                                    1e-8, op0=ALU.mult, op1=ALU.max))
                stq(v.tensor_scalar(qt_t[k][:, :], qt_s[k][:, :], 0.5, None,
                                    op0=ALU.mult))
                stq(v.tensor_scalar(qt_nt[k][:, :], qt_s[k][:, :], -0.5,
                                    None, op0=ALU.mult))
                stq(v.tensor_tensor(
                    qt_c[k][:, :].rearrange("p (g c) -> p g c", c=64),
                    buf[:, :].rearrange("p (g c) -> p g c", c=64),
                    qt_t[k][:, :].unsqueeze(2).broadcast_to([128, 32, 64]),
                    op=ALU.min))
                stq(v.tensor_tensor(
                    qt_c[k][:, :].rearrange("p (g c) -> p g c", c=64),
                    qt_c[k][:, :].rearrange("p (g c) -> p g c", c=64),
                    qt_nt[k][:, :].unsqueeze(2).broadcast_to([128, 32, 64]),
                    op=ALU.max))
                v.tensor_tensor(qt_u[k][:, :], buf[:, :], qt_c[k][:, :],
                                op=ALU.subtract).then_inc(S("vQu"), 1)
            tile_A.n = 0

            def tile_B(v, i):
                k = i % 2
                buf = wb[k]
                v.wait_ge(S("aSG"), i + 1)
                def stq(instr):
                    instr.then_inc(S("vQ"), 1)
                    tile_A.n += 1
                    v.wait_ge(S("vQ"), tile_A.n)
                stq(v.tensor_tensor(
                    qt_w[k][:, :].rearrange("p (g c) -> p g c", c=64),
                    qt_q[k][:, :].rearrange("p (g c) -> p g c", c=64),
                    qt_s[k][:, :].unsqueeze(2).broadcast_to([128, 32, 64]),
                    op=ALU.mult))
                stq(v.tensor_tensor(qt_c[k][:, :], qt_w[k][:, :], buf[:, :],
                                    op=ALU.subtract))
                if i >= 2:
                    v.wait_ge(S("peTP"), 4 * (i - 1))
                stq(v.tensor_tensor(qt_r[k][:, :], buf[:, :], qt_c[k][:, :],
                                    op=ALU.add))

            with nc.Block(no_gpsimd_drain=True) as blk:
                @blk.sync
                def _(sync):
                    for i in range(NT):
                        if i >= 2:
                            sync.wait_ge(S("aWB"), i - 1)
                        if i < 16:
                            sync.dma_start(
                                out=wf[i % 2][:, :],
                                in_=wgu_in[i * 128:(i + 1) * 128, :]
                            ).then_inc(S("dW%d" % (i % 2)), 16)
                        else:
                            j = i - 16
                            sync.dma_start(
                                out=wf[i % 2][:, :].rearrange(
                                    "p (c f) -> p c f", f=128),
                                in_=wpj_in[:, j * 128:(j + 1) * 128].rearrange(
                                    "(c p) f -> p c f", p=128)
                            ).then_inc(S("dW%d" % (i % 2)), 16)

                @blk.gpsimd
                def _(g):
                    g.wait_ge(S("dSG0"), 32)
                    g.wait_ge(S("dSG1"), 32)
                    g.collective_compute(
                        "AllGather", ALU.bypass,
                        replica_groups=[list(range(8))],
                        ins=[sh_wgu.ap().opt()], outs=[g_wgu.ap().opt()]
                    ).then_inc(S("sFF"), 1)
                    g.wait_ge(S("dSG0"), 16 * 6)
                    g.wait_ge(S("dSG1"), 16 * 6)
                    g.collective_compute(
                        "AllGather", ALU.bypass,
                        replica_groups=[list(range(8))],
                        ins=[sh_wpj.ap().opt()], outs=[g_wpj.ap().opt()]
                    ).then_inc(S("sFF"), 1)

                @blk.vector
                def _(v):
                    for i in range(NT):
                        tile_A(v, i)
                        if i >= 1:
                            tile_B(v, i - 1)
                    tile_B(v, NT - 1)

                shu = sh_wgu.ap().rearrange("p (c f) -> p c f", f=2048)
                shp = sh_wpj.ap().rearrange("p (c f) -> p c f", f=2048)

                def _evacs(s, i):
                    for b in range(4):
                        tb = 4 * i + b
                        s.wait_ge(S("peTP"), tb + 1)
                        if i < 16:
                            gidx = i // 4
                            if b == 0 and i % 4 == 0 and gidx >= 2:
                                s.wait_ge(S("dSG%d" % (gidx % 2)),
                                          16 * (gidx // 2))
                            dst = stg[gidx % 2][:, 4 * b:4 * b + 4,
                                                (i % 4) * 128:(i % 4) * 128 + 128]
                        else:
                            j = i - 16
                            if b == 0:
                                s.wait_ge(S("dSG%d" % (j % 2)),
                                          16 * (2 + j // 2))
                            dst = stg[j % 2][:, 4 * b:4 * b + 4, 0:128]
                        s.activation(dst,
                                     ptp[tb % 2][:, :].rearrange(
                                         "p (c f) -> p c f", f=128),
                                     AF.Identity).then_inc(S("aTW"), 1)
                    if i < 16 and i % 4 == 3:
                        gidx = i // 4
                        s.wait_ge(S("aTW"), 16 * (gidx + 1))
                        s.dma_start(
                            out=shu[:, :, gidx * 512:(gidx + 1) * 512],
                            in_=stg[gidx % 2][:, :, :]
                        ).then_inc(S("dSG%d" % (gidx % 2)), 16)
                    elif i >= 16:
                        j = i - 16
                        s.wait_ge(S("aTW"), 4 * (i + 1))
                        s.dma_start(
                            out=shp[:, j, :].rearrange("p (c f) -> p c f",
                                                       f=128),
                            in_=stg[j % 2][:, :, 0:128]
                        ).then_inc(S("dSG%d" % (j % 2)), 16)

                @blk.scalar
                def _(s):
                    for i in range(NT + 1):
                        if i >= 1:
                            s.wait_ge(S("vQu"), i)
                            s.activation(qt_q[(i - 1) % 2][:, :],
                                         qt_u[(i - 1) % 2][:, :],
                                         AF.Sign).then_inc(S("aSG"), 1)
                        if i < NT:
                            s.wait_ge(S("dW%d" % (i % 2)), 16 * (i // 2 + 1))
                            if i >= 2:
                                s.wait_ge(S("vQ"), vq_after_B(i - 2))
                            s.activation(wb[i % 2][:, :], wf[i % 2][:, :],
                                         AF.Identity).then_inc(S("aWB"), 1)
                        if i >= 2:
                            _evacs(s, i - 2)
                    _evacs(s, NT - 1)

                @blk.tensor
                def _(t):
                    for i in range(NT):
                        t.wait_ge(S("vQ"), vq_after_B(i))
                        for b in range(4):
                            tb = 4 * i + b
                            if tb >= 2:
                                t.wait_ge(S("aTW"), tb - 1)
                            for c4 in range(4):
                                dc = b * 4 + c4
                                mm = t.transpose(
                                    ptp[tb % 2][:, c4 * 128:(c4 + 1) * 128],
                                    qt_r[i % 2][:, dc * 128:(dc + 1) * 128],
                                    identb[:, :])
                                if c4 == 3:
                                    mm.then_inc(S("peTP"), 1)


        if PHASES <= 2:
            return nc

        stP4out = st0.enter_context(ExitStack())
        qw = [sb("qw0", [128, 16, 512], BF16, stP4out),
              sb("qw1", [128, 16, 512], BF16, stP4out)]
        with ExitStack() as stP4a:
            kp_tm = sb("kp_tm", [128, TC, D], BF16, stP4a)
            v_ext = sb("v_ext", [128, TC, HEADS, 132], BF16, stP4a)
            wwin = [sb("wwin0", [128, 16, 512], BF16, stP4a),
                    sb("wwin1", [128, 16, 512], BF16, stP4a)]
            scrm = [sb("scrm0", [128, 512], F32, stP4a)] * 2
            scre = [sb("scre0", [128, 512], F32, stP4a)] * 2
            scrr = [sb("scrr0", [128, 512], F32, stP4a)] * 2
            pk = [pst("pk0", [128, 512], stP4a), pst("pk1", [128, 512], stP4a)]
            pkv = [pst("pkv%d" % i, [128, 396], stP4a) for i in range(6)]

            def kv_slot(h):
                return pkv[h // 3][:, (h % 3) * 132:(h % 3) * 132 + 129]

            with nc.Block(no_gpsimd_drain=True) as blk:
                @blk.gpsimd
                def _(g):
                    g.memset(v_ext[:, :, :, 128:129], 1.0).then_inc(S("vONE"), 1)
                    g.wait_ge(S("sAF"), 1)
                    for wi in range(4, 8):
                        w = wi % 4
                        rank = 4 + w // 2
                        off = (w % 2) * 512
                        g.wait_ge(S("pePJ"), (wi - 1) * TC)
                        g.dma_start(
                            out=wwin[wi % 2][:, :, :],
                            in_=gv_att[rank, :, :, off:off + 512]
                        ).then_inc(S("dVW%d" % (wi % 2)), 16)

                @blk.sync
                def _(sync):
                    sync.wait_ge(S("sAF"), 1)
                    for wi in range(4):
                        rank = 2 + wi // 2
                        off = (wi % 2) * 512
                        if wi >= 2:
                            sync.wait_ge(S("pePJ"), (wi - 1) * TC)
                        sync.dma_start(
                            out=wwin[wi % 2][:, :, :],
                            in_=gv_att[rank, :, :, off:off + 512]
                        ).then_inc(S("dKW%d" % (wi % 2)), 16)
                    for j in range(2):
                        sync.dma_start(
                            out=qw[j][:, :, :],
                            in_=gv_att[0, :, :, j * 512:j * 512 + 512]
                        ).then_inc(S("dQW%d" % (j % 2)), 16)

                @blk.tensor
                def _(t):
                    for wi in range(8):
                        is_k = wi < 4
                        w = wi % 4
                        buf = wwin[wi % 2]
                        bh, bl = (kbr_h, kbr_l) if is_k else (vbr_h, vbr_l)
                        if is_k:
                            t.wait_ge(S("dKW%d" % (wi % 2)), 16 * (wi // 2 + 1))
                        else:
                            t.wait_ge(S("dVW%d" % (wi % 2)),
                                      16 * ((wi - 4) // 2 + 1))
                        for tc in range(TC):
                            gi = wi * TC + tc
                            pb = pk[gi % 2]
                            if gi >= 2:
                                t.wait_ge(S("vKP"), gi - 1)
                            t.matmul(pb[:, :], ones_r[0:1, 0:128],
                                     bh[0:1, w * 512:(w + 1) * 512],
                                     start=True, stop=False)
                            t.matmul(pb[:, :], ones_r[0:1, 0:128],
                                     bl[0:1, w * 512:(w + 1) * 512],
                                     start=False, stop=False)
                            for dc in range(NDC):
                                mm = t.matmul(pb[:, :],
                                              xn_fm[:, dc, tc * 128:(tc + 1) * 128],
                                              buf[:, dc, :], start=False,
                                              stop=(dc == NDC - 1))
                                if dc == NDC - 1:
                                    mm.then_inc(S("pePJ"), 1)
                    t.wait_ge(S("vKP"), 64)
                    t.wait_ge(S("vONE"), 1)
                    for tc in range(TC):
                        for h in range(HEADS):
                            mm = t.matmul(kv_slot(h),
                                          kp_tm[:, tc, h * 128:(h + 1) * 128],
                                          v_ext[:, tc, h, 0:129],
                                          start=(tc == 0 and h % 3 == 0),
                                          stop=(tc == TC - 1),
                                          skip_group_check=True)
                            if tc == TC - 1 and h == HEADS - 1:
                                mm.then_inc(S("peKVD"), 1)

                @blk.vector
                def _(v):
                    for gi in range(64):
                        wi, tc = divmod(gi, TC)
                        is_k = wi < 4
                        w = wi % 4
                        pb = pk[gi % 2]
                        par = gi % 2
                        v.wait_ge(S("pePJ"), gi + 1)
                        if is_k:
                            v.tensor_scalar(scrm[par][:, :], pb[:, :], 0.0,
                                            None, op0=ALU.min
                                            ).then_inc(S("vMIN"), 1)
                            v.wait_ge(S("aER"), 2 * (gi + 1))
                            v.tensor_tensor(
                                kp_tm[:, tc, w * 512:(w + 1) * 512],
                                scre[par][:, :], scrr[par][:, :], op=ALU.add
                            ).then_inc(S("vKP"), 1)
                        else:
                            v.tensor_copy(
                                v_ext[:, tc, 4 * w:4 * w + 4, 0:128],
                                pb[:, :].rearrange("p (h c) -> p h c", c=128)
                            ).then_inc(S("vKP"), 1)
                    v.wait_ge(S("peKVD"), 1)
                    for h in range(HEADS):
                        v.tensor_copy(kv_f[:, h * 132:h * 132 + 129],
                                      kv_slot(h)).then_inc(S("vKVF"), 1)
                    for h in range(HEADS):
                        mm = v.memset(kv_f[:, h * 132 + 129:h * 132 + 132], 0.0)
                        if h == HEADS - 1:
                            mm.then_inc(S("vKVF"), 1)

                @blk.scalar
                def _(s):
                    for gi in range(32):
                        par = gi % 2
                        s.wait_ge(S("vMIN"), gi + 1)
                        s.activation(scre[par][:, :], scrm[par][:, :], AF.Exp
                                     ).then_inc(S("aER"), 1)
                        s.wait_ge(S("pePJ"), gi + 1)
                        s.activation(scrr[par][:, :], pk[gi % 2][:, :], AF.Relu
                                     ).then_inc(S("aER"), 1)

        if PHASES <= 3:
            return nc

        with ExitStack() as stP4b:
            qp_fm = sb("qp_fm", [128, HEADS, T], BF16, stP4b)
            attn_f = qp_fm
            scrm = [sb("scrm0b", [128, 512], F32, stP4b),
                    sb("scrm1b", [128, 512], F32, stP4b)]
            scre = [sb("scre0b", [128, 512], F32, stP4b),
                    sb("scre1b", [128, 512], F32, stP4b)]
            scrr = [sb("scrr0b", [128, 512], F32, stP4b),
                    sb("scrr1b", [128, 512], F32, stP4b)]
            numb = [sb("numb0", [128, 512], BF16, stP4b),
                    sb("numb1", [128, 512], BF16, stP4b)]
            zrow = [sb("zrow0", [1, 512], F32, stP4b),
                    sb("zrow1", [1, 512], F32, stP4b)]
            zbf = [sb("zbf0", [1, 512], BF16, stP4b),
                   sb("zbf1", [1, 512], BF16, stP4b)]
            owin = [sb("owin0", [128, 16, 512], BF16, stP4b),
                    sb("owin1", [128, 16, 512], BF16, stP4b)]
            x_re = sb("x_re", [128, 8, 512], F32, stP4b)
            pev = [sb("pev0", [128, 512], F32, stP4b),
                   sb("pev1", [128, 512], F32, stP4b)]
            pq = [pst("pq0", [128, 512], stP4b), pst("pq1", [128, 512], stP4b)]
            pn = [pst("pn0", [128, 512], stP4b), pst("pn1", [128, 512], stP4b)]
            pz = [pst("pz0", [1, 512], stP4b), pst("pz1", [1, 512], stP4b)]
            pr = [pst("pr0", [128, 512], stP4b), pst("pr1", [128, 512], stP4b)]

            with nc.Block() as blk:
                @blk.gpsimd
                def _(g):
                    g.wait_ge(S("vKVF"), 17)
                    g.dma_start(out=ar_i.ap(), in_=kv_f[:, :]).then_inc(S("dAR"), 16)
                    g.wait_ge(S("dAR"), 16)
                    g.collective_compute(
                        "AllReduce", ALU.add,
                        replica_groups=[[0, 1, 2, 3], [4, 5, 6, 7]],
                        ins=[ar_i.ap().opt()], outs=[ar_o.ap().opt()]
                    ).then_inc(S("sAR"), 1)
                    g.wait_ge(S("sAR"), 1)
                    g.dma_start(out=kv_f[:, :], in_=ar_o.ap()).then_inc(S("dAR"), 16)

                @blk.sync
                def _(sync):
                    for j in range(2, 4):
                        rank = j // 2
                        off = (j % 2) * 512
                        sync.wait_ge(S("peQ"), 8 * (j - 1))
                        sync.dma_start(
                            out=qw[j % 2][:, :, :],
                            in_=gv_att[rank, :, :, off:off + 512]
                        ).then_inc(S("dQW%d" % (j % 2)), 16)
                    for w in range(NW):
                        rank = 6 + w // 2
                        off = (w % 2) * 512
                        if w >= 2:
                            sync.wait_ge(S("peO"), (w - 1) * TC)
                        sync.dma_start(
                            out=owin[w % 2][:, :, :],
                            in_=gv_att[rank, :, :, off:off + 512]
                        ).then_inc(S("dOW%d" % (w % 2)), 16)
                        if w >= 1:
                            sync.wait_ge(S("vPART"), w * TC)
                        sync.dma_start(
                            out=x_re[:, :, :],
                            in_=x_in[:, w * 512:(w + 1) * 512].rearrange(
                                "(c p) f -> p c f", p=128)
                        ).then_inc(S("dXR"), 16)
                        for tc in range(TC):
                            gi = w * TC + tc
                            sync.wait_ge(S("vPART"), gi + 1)
                            sync.dma_start(
                                out=part_nat.ap()[tc * 128:(tc + 1) * 128,
                                                  w * 512:(w + 1) * 512],
                                in_=pev[tc % 2][:, :]
                            ).then_inc(S("dPT%d" % (tc % 2)), 16)
                    sync.wait_ge(S("dPT0"), 16 * 16)
                    sync.wait_ge(S("dPT1"), 16 * 16)

                @blk.tensor
                def _(t):
                    for m in range(16):
                        j = m // 4
                        t.wait_ge(S("dQW%d" % (j % 2)), 16 * (j // 2 + 1))
                        for th in range(2):
                            gi = m * 2 + th
                            pb = pq[gi % 2]
                            if gi >= 2:
                                t.wait_ge(S("vQP"), gi - 1)
                            for dc in range(NDC):
                                mm = t.matmul(pb[:, :],
                                              qw[j % 2][:, dc,
                                                        (m % 4) * 128:
                                                        (m % 4) * 128 + 128],
                                              xn_fm[:, dc,
                                                    th * 512:th * 512 + 512],
                                              start=(dc == 0),
                                              stop=(dc == NDC - 1))
                                if dc == NDC - 1:
                                    mm.then_inc(S("peQ"), 1)
                    t.wait_ge(S("vKVB"), 1)
                    for gi in range(33):
                        if gi < 32:
                            h, th = divmod(gi, 2)
                            par = gi % 2
                            if gi >= 2:
                                t.wait_ge(S("vAT"), gi - 1)
                            t.wait_ge(S("vQP"), gi + 1)
                            t.matmul(pn[par][:, :],
                                     kv_b[:, h * 132:h * 132 + 128],
                                     qp_fm[:, h, th * 512:th * 512 + 512],
                                     start=True, stop=True)
                            t.matmul(pz[par][:, :],
                                     kv_b[:, h * 132 + 128:h * 132 + 129],
                                     qp_fm[:, h, th * 512:th * 512 + 512],
                                     start=True, stop=True).then_inc(S("peN"), 1)
                        if gi >= 1:
                            pj = gi - 1
                            t.wait_ge(S("vZB"), pj + 1)
                            t.matmul(pr[pj % 2][:, :], ones_r[0:1, 0:128],
                                     zbf[pj % 2][0:1, :], start=True, stop=True
                                     ).then_inc(S("peR"), 1)
                    t.wait_ge(S("vAT"), 32)
                    for w in range(NW):
                        t.wait_ge(S("dOW%d" % (w % 2)), 16 * (w // 2 + 1))
                        for tc in range(TC):
                            gi = w * TC + tc
                            pb = pq[gi % 2]
                            if gi >= 2:
                                t.wait_ge(S("vPART"), gi - 1)
                            t.matmul(pb[:, :], ones_r[0:1, 0:128],
                                     obr_h[0:1, w * 512:(w + 1) * 512],
                                     start=True, stop=False)
                            t.matmul(pb[:, :], ones_r[0:1, 0:128],
                                     obr_l[0:1, w * 512:(w + 1) * 512],
                                     start=False, stop=False)
                            for fc in range(NDC):
                                mm = t.matmul(pb[:, :],
                                              attn_f[:, fc,
                                                     tc * 128:tc * 128 + 128],
                                              owin[w % 2][:, fc, :],
                                              start=False,
                                              stop=(fc == NDC - 1))
                                if fc == NDC - 1:
                                    mm.then_inc(S("peO"), 1)

                @blk.vector
                def _(v):
                    for gi in range(32):
                        m, th = divmod(gi, 2)
                        par = gi % 2
                        v.wait_ge(S("peQ"), gi + 1)
                        v.tensor_scalar(scrm[par][:, :], pq[par][:, :],
                                        qbc[:, m:m + 1], 0.0, op0=ALU.add,
                                        op1=ALU.min).then_inc(S("vQM"), 1)
                        v.wait_ge(S("aQE"), 2 * (gi + 1))
                        v.tensor_tensor(qp_fm[:, m, th * 512:th * 512 + 512],
                                        scre[par][:, :], scrr[par][:, :],
                                        op=ALU.add).then_inc(S("vQP"), 1)
                    v.wait_ge(S("dAR"), 32)
                    v.tensor_copy(kv_b[:, :], kv_f[:, :]).then_inc(S("vKVB"), 1)
                    for gi in range(32):
                        h, th = divmod(gi, 2)
                        par = gi % 2
                        v.wait_ge(S("peN"), gi + 1)
                        v.tensor_scalar(zrow[par][:, :], pz[par][:, :],
                                        ATT_EPS, None, op0=ALU.add
                                        ).then_inc(S("vZ1"), 1)
                        v.wait_ge(S("vZ1"), gi + 1)
                        v.reciprocal(zrow[par][:, :], zrow[par][:, :]
                                     ).then_inc(S("vZ2"), 1)
                        v.wait_ge(S("vZ2"), gi + 1)
                        v.tensor_copy(zbf[par][:, :], zrow[par][:, :]
                                      ).then_inc(S("vZB"), 1)
                        v.tensor_copy(numb[par][:, :], pn[par][:, :]
                                      ).then_inc(S("vNB"), 1)
                        v.wait_ge(S("vNB"), gi + 1)
                        v.wait_ge(S("peR"), gi + 1)
                        v.tensor_tensor(attn_f[:, h, th * 512:th * 512 + 512],
                                        numb[par][:, :], pr[par][:, :],
                                        op=ALU.mult).then_inc(S("vAT"), 1)
                    for w in range(NW):
                        v.wait_ge(S("dXR"), 16 * (w + 1))
                        for tc in range(TC):
                            gi = w * TC + tc
                            v.wait_ge(S("peO"), gi + 1)
                            if gi >= 2:
                                v.wait_ge(S("dPT%d" % (tc % 2)), 16 * (gi // 2))
                            v.tensor_tensor(pev[tc % 2][:, :], pq[gi % 2][:, :],
                                            x_re[:, tc, :], op=ALU.add
                                            ).then_inc(S("vPART"), 1)

                @blk.scalar
                def _(s):
                    for gi in range(32):
                        m = gi // 2
                        par = gi % 2
                        s.wait_ge(S("vQM"), gi + 1)
                        s.activation(scre[par][:, :], scrm[par][:, :], AF.Exp
                                     ).then_inc(S("aQE"), 1)
                        s.wait_ge(S("peQ"), gi + 1)
                        s.activation(scrr[par][:, :], pq[par][:, :], AF.Relu,
                                     bias=qbc[:, m:m + 1]).then_inc(S("aQE"), 1)

        if PHASES <= 4:
            return nc
        stP4out.close()
        stB.close()

        with ExitStack() as stP5:
            h_fm = sb("h_fm", [128, NHC, T], BF16, stP5)
            ones_cb = sb("ones_cb", [128, 1], BF16, stP5)
            s8 = sb("s8", [128, 8], F32, stP5)

            with ExitStack() as stP5a:
                gw = [sb("gw0", [128, 16, 256], BF16, stP5a),
                      sb("gw1", [128, 16, 256], BF16, stP5a)]
                uw = [sb("uw0", [128, 16, 256], BF16, stP5a),
                      sb("uw1", [128, 16, 256], BF16, stP5a)]
                silu_sb = sb("silu_sb", [128, 1024], BF16, stP5a)
                sqs = [sb("sqs0", [128, 1024], BF16, stP5a),
                       sb("sqs1", [128, 1024], BF16, stP5a)]
                pga = [pst("pga0", [128, 512], stP5a),
                       pst("pga1", [128, 512], stP5a)]
                pua = [pst("pua0", [128, 512], stP5a),
                       pst("pua1", [128, 512], stP5a)]
                pS = pst("pS", [128, 512], stP5a)

                with nc.Block() as blk:
                    @blk.gpsimd
                    def _(g):
                        g.memset(ones_cb[:, :], 1.0).then_inc(S("vONE2"), 1)
                        g.wait_ge(S("sFF"), 1)
                        for bidx in range(32):
                            rank_g = bidx // 8
                            off = (bidx % 8) * 256
                            if bidx >= 2:
                                g.wait_ge(S("peU"), 2 * (bidx - 1))
                            g.dma_start(
                                out=uw[bidx % 2][:, :, :],
                                in_=gv_wgu[rank_g + 4, :, :, off:off + 256]
                            ).then_inc(S("dUW%d" % (bidx % 2)), 16)

                    @blk.sync
                    def _(sync):
                        sync.wait_ge(S("sFF"), 1)
                        for bidx in range(32):
                            rank_g = bidx // 8
                            off = (bidx % 8) * 256
                            if bidx >= 2:
                                sync.wait_ge(S("peU"), 2 * (bidx - 1))
                            sync.dma_start(
                                out=gw[bidx % 2][:, :, :],
                                in_=gv_wgu[rank_g, :, :, off:off + 256]
                            ).then_inc(S("dGW%d" % (bidx % 2)), 16)

                    @blk.tensor
                    def _(t):
                        t.wait_ge(S("vONE2"), 1)
                        for hc in range(NHC):
                            bidx = hc // 2
                            j = hc % 2
                            t.wait_ge(S("dGW%d" % (bidx % 2)),
                                      16 * (bidx // 2 + 1))
                            t.wait_ge(S("dUW%d" % (bidx % 2)),
                                      16 * (bidx // 2 + 1))
                            if hc >= 1:
                                t.wait_ge(S("aSL"), 2 * hc)
                            for th in range(2):
                                for dc in range(NDC):
                                    mm = t.matmul(pga[th][:, :],
                                                  gw[bidx % 2][:, dc,
                                                               j * 128:j * 128 + 128],
                                                  xn_fm[:, dc,
                                                        th * 512:th * 512 + 512],
                                                  start=(dc == 0),
                                                  stop=(dc == NDC - 1))
                                    if th == 1 and dc == NDC - 1:
                                        mm.then_inc(S("peG"), 1)
                            if hc >= 1:
                                t.wait_ge(S("vH"), 2 * hc)
                            for th in range(2):
                                for dc in range(NDC):
                                    mm = t.matmul(pua[th][:, :],
                                                  uw[bidx % 2][:, dc,
                                                               j * 128:j * 128 + 128],
                                                  xn_fm[:, dc,
                                                        th * 512:th * 512 + 512],
                                                  start=(dc == 0),
                                                  stop=(dc == NDC - 1))
                                    if th == 1 and dc == NDC - 1:
                                        mm.then_inc(S("peU"), 1)
                            if hc >= 1:
                                t.wait_ge(S("aSQ5"), hc)
                                sq = sqs[(hc - 1) % 2]
                                for tc in range(TC):
                                    mm = t.matmul(pS[:, (hc - 1) + tc * 64:
                                                     (hc - 1) + tc * 64 + 1],
                                                  sq[:, tc * 128:tc * 128 + 128],
                                                  ones_cb[:, :],
                                                  start=(hc - 1 == 0 and tc == 0),
                                                  stop=False,
                                                  skip_group_check=True)
                                    if tc == TC - 1:
                                        mm.then_inc(S("peS"), 1)
                        t.wait_ge(S("aSQ5"), NHC)
                        sq = sqs[(NHC - 1) % 2]
                        for tc in range(TC):
                            mm = t.matmul(pS[:, (NHC - 1) + tc * 64:
                                             (NHC - 1) + tc * 64 + 1],
                                          sq[:, tc * 128:tc * 128 + 128],
                                          ones_cb[:, :], start=False,
                                          stop=(tc == TC - 1),
                                          skip_group_check=True)
                            if tc == TC - 1:
                                mm.then_inc(S("peS"), 1)

                    @blk.scalar
                    def _(s):
                        for hc in range(NHC):
                            s.wait_ge(S("peG"), hc + 1)
                            for th in range(2):
                                s.activation(
                                    silu_sb[:, th * 512:th * 512 + 512],
                                    pga[th][:, :], AF.Silu).then_inc(S("aSL"), 1)
                            s.wait_ge(S("vH"), 2 * (hc + 1))
                            if hc >= 2:
                                s.wait_ge(S("peS"), hc - 1)
                            s.activation(sqs[hc % 2][:, :], h_fm[:, hc, :],
                                         AF.Square).then_inc(S("aSQ5"), 1)
                        s.wait_ge(S("vRS"), 1)
                        s.activation(s8[:, :], s8[:, :], AF.Sqrt,
                                     scale=1.0 / 8192.0, bias=eps_rms[:, :]
                                     ).then_inc(S("aRS"), 1)

                    @blk.vector
                    def _(v):
                        for hc in range(NHC):
                            v.wait_ge(S("peU"), hc + 1)
                            v.wait_ge(S("aSL"), 2 * (hc + 1))
                            for th in range(2):
                                v.tensor_tensor(
                                    h_fm[:, hc, th * 512:th * 512 + 512],
                                    silu_sb[:, th * 512:th * 512 + 512],
                                    pua[th][:, :], op=ALU.mult
                                ).then_inc(S("vH"), 1)
                        v.wait_ge(S("peS"), NHC)
                        v.tensor_reduce(
                            s8[:, :],
                            pS[:, :].rearrange("p (t h) -> p t h", h=64),
                            axis=AX.X, op=ALU.add).then_inc(S("vRS"), 1)
                        v.wait_ge(S("aRS"), 1)
                        v.reciprocal(rstd_f[:, :], s8[:, :]).then_inc(S("vRS"), 1)

            with ExitStack() as stP5c:
                pw = [sb("pw%d" % i, [128, 4, 512], BF16, stP5c)
                      for i in range(4)]
                part_re = sb("part_re", [128, 8, 512], F32, stP5c)
                ev1 = [sb("ev1%d" % i, [128, 512], F32, stP5c)
                       for i in range(4)]
                pp = [pst("pp%d" % i, [128, 512], stP5c) for i in range(8)]

                with nc.Block() as blk:
                    @blk.sync
                    def _(sync):
                        sync.wait_ge(S("sFF"), 2)
                        sync.wait_ge(S("dPT0"), 16 * 16)
                        sync.wait_ge(S("dPT1"), 16 * 16)
                        for w in range(NW):
                            if w >= 1:
                                sync.wait_ge(S("vE1"), w * TC)
                            sync.dma_start(
                                out=part_re[:, :, :],
                                in_=part_nat.ap()[:, w * 512:(w + 1) * 512]
                                .rearrange("(c p) f -> p c f", p=128)
                            ).then_inc(S("dPR"), 16)
                            for blkk in range(0, 16, 2):
                                gbi = w * 16 + blkk
                                rb = blkk // 2
                                if gbi >= 4:
                                    sync.wait_ge(S("pePB"), gbi - 3)
                                sync.dma_start(
                                    out=pw[gbi % 4][:, :, :],
                                    in_=gv_wpj[rb, :, 0:4,
                                               w * 512:(w + 1) * 512]
                                ).then_inc(S("dPW%d" % (gbi % 4)), 16)
                            for tc in range(TC):
                                gi = w * TC + tc
                                sync.wait_ge(S("vE1"), gi + 1)
                                sync.dma_start(
                                    out=out_d[tc * 128:(tc + 1) * 128,
                                              w * 512:(w + 1) * 512],
                                    in_=ev1[tc % 4][:, :]
                                ).then_inc(S("dOUT%d" % (tc % 4)), 16)
                        for k in range(4):
                            sync.wait_ge(S("dOUT%d" % k), 16 * 8)

                    @blk.gpsimd
                    def _(g):
                        g.wait_ge(S("sFF"), 2)
                        for w in range(NW):
                            for blkk in range(1, 16, 2):
                                gbi = w * 16 + blkk
                                rb = blkk // 2
                                if gbi >= 4:
                                    g.wait_ge(S("pePB"), gbi - 3)
                                g.dma_start(
                                    out=pw[gbi % 4][:, :, :],
                                    in_=gv_wpj[rb, :, 4:8,
                                               w * 512:(w + 1) * 512]
                                ).then_inc(S("dPW%d" % (gbi % 4)), 16)

                    @blk.tensor
                    def _(t):
                        for w in range(NW):
                            for hc in range(NHC):
                                gbi = w * 16 + hc // 4
                                t.wait_ge(S("dPW%d" % (gbi % 4)),
                                          16 * (gbi // 4 + 1))
                                for tc in range(TC):
                                    if w >= 1 and hc == 0:
                                        t.wait_ge(S("vE1"), w * TC + tc + 1 - TC)
                                    mm = t.matmul(pp[tc][:, :],
                                                  h_fm[:, hc,
                                                       tc * 128:tc * 128 + 128],
                                                  pw[gbi % 4][:, hc % 4, :],
                                                  start=(hc == 0),
                                                  stop=(hc == NHC - 1))
                                    if hc == NHC - 1 and tc != TC - 1:
                                        mm.then_inc(S("peTCD"), 1)
                                    if hc % 4 == 3 and tc == TC - 1:
                                        mm.then_inc(S("pePB"), 1)

                    @blk.vector
                    def _(v):
                        v.wait_ge(S("vRS"), 2)
                        for w in range(NW):
                            v.wait_ge(S("dPR"), 16 * (w + 1))
                            for tc in range(TC):
                                gi = w * TC + tc
                                if gi >= 4:
                                    v.wait_ge(S("dOUT%d" % (tc % 4)),
                                              16 * (gi // 4))
                                if tc == TC - 1:
                                    v.wait_ge(S("pePB"), 16 * (w + 1))
                                else:
                                    v.wait_ge(S("peTCD"),
                                              w * (TC - 1) + tc + 1)
                                v.scalar_tensor_tensor(
                                    ev1[tc % 4][:, :], pp[tc][:, :],
                                    rstd_f[:, tc:tc + 1], part_re[:, tc, :],
                                    op0=ALU.mult, op1=ALU.add
                                ).then_inc(S("vE1"), 1)
    return nc


_NC_CACHE = {}


def kernel(L0_attn_hidden_states, L0_attn_q_w, L0_attn_q_b, L0_attn_k_w,
           L0_attn_k_b, L0_attn_v_w, L0_attn_v_b, L0_attn_o_w, L0_attn_o_b,
           L0_ffn_w_gate_up, L0_ffn_w_proj):
    x = np.ascontiguousarray(np.asarray(L0_attn_hidden_states, dtype=np.float32))
    ws = [np.ascontiguousarray(np.asarray(w, dtype=np.float32))
          for w in (L0_attn_q_w, L0_attn_k_w, L0_attn_v_w, L0_attn_o_w)]
    wgu = np.ascontiguousarray(np.asarray(L0_ffn_w_gate_up, dtype=np.float32))
    wpj = np.ascontiguousarray(np.asarray(L0_ffn_w_proj, dtype=np.float32))
    qb = np.ascontiguousarray(
        np.asarray(L0_attn_q_b, np.float32).reshape(16, 128).T)
    kb = np.ascontiguousarray(np.asarray(L0_attn_k_b, np.float32).reshape(1, D))
    vb = np.ascontiguousarray(np.asarray(L0_attn_v_b, np.float32).reshape(1, D))
    ob = np.ascontiguousarray(np.asarray(L0_attn_o_b, np.float32).reshape(1, D))

    ident = np.eye(128, dtype=np.float32)
    in_maps = []
    for c in range(8):
        b, qd = c // 4, c % 4
        wi, half = c // 2, c % 2
        in_maps.append({
            "x_slice": np.ascontiguousarray(x[b, qd * T:(qd + 1) * T, :]),
            "attw_slice": np.ascontiguousarray(
                ws[wi][half * 1024:(half + 1) * 1024, :]),
            "wgu_slice": np.ascontiguousarray(wgu[c * 2048:(c + 1) * 2048, :]),
            "wproj_slice": np.ascontiguousarray(wpj[:, c * 1024:(c + 1) * 1024]),
            "qb_col": qb, "kb_row": kb, "vb_row": vb, "ob_row": ob,
            "ident_in": ident,
        })

    if "nc" not in _NC_CACHE:
        _NC_CACHE["nc"] = build()
    res = run_bass_kernel_spmd(_NC_CACHE["nc"], in_maps, core_ids=list(range(8)))
    out = np.empty((2, 4096, D), dtype=np.float32)
    for c in range(8):
        b, qd = c // 4, c % 4
        out[b, qd * T:(qd + 1) * T, :] = np.asarray(res.results[c]["out_slice"])
    return out


# version 15 (final)
# speedup vs baseline: 1.0379x; 1.0379x over previous
import numpy as np
import concourse.bass as bass
import concourse.mybir as mybir
from concourse.bass_utils import run_bass_kernel_spmd

F32 = mybir.dt.float32
BF16 = mybir.dt.bfloat16
AF = mybir.ActivationFunctionType
ALU = mybir.AluOpType
AX = mybir.AxisListType

T = 1024
TC = 8
D = 2048
NDC = 16
NHC = 64
NW = 4
HEADS = 16
LN_EPS = 1e-5
RMS_EPS = 1.1920929e-07
ATT_EPS = 1e-6
PHASES = 9


def build():
    nc = bass.Bass("TRN2", target_bir_lowering=False)

    x_in = nc.dram_tensor("x_slice", [T, D], F32, kind="ExternalInput").ap()
    attw = nc.dram_tensor("attw_slice", [1024, D], F32, kind="ExternalInput").ap()
    wgu_in = nc.dram_tensor("wgu_slice", [2048, D], F32, kind="ExternalInput").ap()
    wpj_in = nc.dram_tensor("wproj_slice", [2048, 1024], F32,
                            kind="ExternalInput").ap()
    qb_col = nc.dram_tensor("qb_col", [128, 16], F32, kind="ExternalInput").ap()
    ident_in = nc.dram_tensor("ident_in", [128, 128], F32, kind="ExternalInput").ap()
    kb_row = nc.dram_tensor("kb_row", [1, D], F32, kind="ExternalInput").ap()
    vb_row = nc.dram_tensor("vb_row", [1, D], F32, kind="ExternalInput").ap()
    ob_row = nc.dram_tensor("ob_row", [1, D], F32, kind="ExternalInput").ap()
    out_d = nc.dram_tensor("out_slice", [T, D], F32, kind="ExternalOutput").ap()

    xn_nat = nc.dram_tensor("xn_nat", [T, D], BF16)
    attw_nat = nc.dram_tensor("attw_nat", [1024, D], BF16)
    sh_att = nc.dram_tensor("sh_att", [128, 16 * 1024], BF16)
    sh_wgu = nc.dram_tensor("sh_wgu", [128, 16 * 2048], BF16)
    sh_wpj = nc.dram_tensor("sh_wpj", [128, 8 * 2048], BF16)
    g_att = nc.dram_tensor("g_att", [1024, 16 * 1024], BF16, addr_space="Shared")
    g_wgu = nc.dram_tensor("g_wgu", [1024, 16 * 2048], BF16, addr_space="Shared")
    g_wpj = nc.dram_tensor("g_wpj", [1024, 8 * 2048], BF16, addr_space="Shared")
    ar_i = nc.dram_tensor("ar_i", [128, HEADS * 132], F32)
    ar_o = nc.dram_tensor("ar_o", [128, HEADS * 132], F32)
    part_nat = nc.dram_tensor("part_nat", [T, D], F32)

    gv_att = g_att.ap().rearrange("(r p) (c f) -> r p c f", p=128, f=1024)
    gv_wgu = g_wgu.ap().rearrange("(r p) (c f) -> r p c f", p=128, f=2048)
    gv_wpj = g_wpj.ap().rearrange("(r p) (c f) -> r p c f", p=128, f=2048)

    from contextlib import ExitStack
    with ExitStack() as st0:
        def sb(name, shape, dt, stk=None):
            return (stk or st0).enter_context(nc.sbuf_tensor(name, shape, dt))

        def pst(name, shape, stk=None, dt=F32):
            return (stk or st0).enter_context(nc.psum_tensor(name, shape, dt))

        sems = {}

        def S(name):
            if name not in sems:
                sems[name] = st0.enter_context(nc.semaphore(name))
            return sems[name]

        xn_fm = sb("xn_fm", [128, NDC, T], BF16)
        eps_ln = sb("eps_ln", [128, 1], F32)
        eps_rms = sb("eps_rms", [128, 1], F32)
        rstd_f = sb("rstd_f", [128, 8], F32)
        stB = st0.enter_context(ExitStack())
        kv_b = sb("kv_b", [128, HEADS * 132], BF16, stB)
        kv_f = sb("kv_f", [128, HEADS * 132], F32, stB)
        qbc = sb("qbc", [128, 16], F32, stB)
        kbr_h = sb("kbr_h", [1, D], BF16, stB)
        kbr_l = sb("kbr_l", [1, D], BF16, stB)
        vbr_h = sb("vbr_h", [1, D], BF16, stB)
        vbr_l = sb("vbr_l", [1, D], BF16, stB)
        obr_h = sb("obr_h", [1, D], BF16, stB)
        obr_l = sb("obr_l", [1, D], BF16, stB)
        ones_r = sb("ones_r", [1, 512], BF16, stB)
        identb = sb("identb", [128, 128], BF16, stB)
        ones3 = sb("ones3", [65, 128], BF16, stB)


        with ExitStack() as stP0:
            kbr = sb("kbr", [1, D], F32, stP0)
            vbr = sb("vbr", [1, D], F32, stP0)
            obr = sb("obr", [1, D], F32, stP0)
            brow_f = sb("brow_f", [1, D], F32, stP0)
            xb = [sb("xb0", [128, D], F32, stP0),
                  sb("xb1", [128, D], F32, stP0)]
            xb2 = [sb("xb20", [128, D], F32, stP0),
                   sb("xb21", [128, D], F32, stP0)]
            xnb = [sb("xnb0", [128, D], BF16, stP0),
                   sb("xnb1", [128, D], BF16, stP0)]
            sq_d = sb("sq_d", [128, D], BF16, stP0)
            sums = sb("sums", [128, 8], F32, stP0)
            sumsq = sb("sumsq", [128, 8], F32, stP0)
            mean8 = sb("mean8", [128, 8], F32, stP0)
            var8 = sb("var8", [128, 8], F32, stP0)
            rstd8 = sb("rstd8", [128, 8], F32, stP0)
            nmr8 = sb("nmr8", [128, 8], F32, stP0)

            with nc.Block() as blk:
                @blk.sync
                def _(sync):
                    sync.dma_start(out=qbc[:, :], in_=qb_col).then_inc(S("dLD"), 16)
                    sync.dma_start(out=kbr[:, :], in_=kb_row).then_inc(S("dLD"), 16)
                    sync.dma_start(out=vbr[:, :], in_=vb_row).then_inc(S("dLD"), 16)
                    sync.dma_start(out=obr[:, :], in_=ob_row).then_inc(S("dLD"), 16)
                    for i in range(TC):
                        if i >= 2:
                            sync.wait_ge(S("vSUM"), i - 1)
                            sync.wait_ge(S("aSQ"), i - 1)
                        sync.dma_start(
                            out=xb[i % 2][:, :],
                            in_=x_in[i * 128:(i + 1) * 128, :]
                        ).then_inc(S("dX1%d" % (i % 2)), 16)
                    for i in range(TC):
                        if i >= 2:
                            sync.wait_ge(S("aXN"), i - 1)
                        sync.dma_start(
                            out=xb2[i % 2][:, :],
                            in_=x_in[i * 128:(i + 1) * 128, :]
                        ).then_inc(S("dX2%d" % (i % 2)), 16)
                        if i >= 1:
                            sync.wait_ge(S("aXN"), i)
                            sync.dma_start(
                                out=xn_nat.ap()[(i - 1) * 128:i * 128, :],
                                in_=xnb[(i - 1) % 2][:, :]
                            ).then_inc(S("dXNs%d" % ((i - 1) % 2)), 16)
                    sync.wait_ge(S("aXN"), TC)
                    sync.dma_start(
                        out=xn_nat.ap()[(TC - 1) * 128:TC * 128, :],
                        in_=xnb[(TC - 1) % 2][:, :]
                    ).then_inc(S("dXNs%d" % ((TC - 1) % 2)), 16)
                    sync.wait_ge(S("dXNs0"), 64)
                    sync.wait_ge(S("dXNs1"), 64)
                    sync.wait_ge(S("dAW"), 16)

                @blk.gpsimd
                def _(g):
                    g.memset(ones_r[:, :], 1.0)
                    g.memset(ones3[:, :], 1.0)
                    g.memset(eps_ln[:, :], LN_EPS)
                    g.memset(eps_rms[:, :], RMS_EPS)
                    g.dma_start(out=attw_nat.ap(), in_=attw
                                ).then_inc(S("dAW"), 16)
                    g.dma_start(out=kbr_h[:, :], in_=kb_row).then_inc(S("dCAST"), 16)
                    g.dma_start(out=vbr_h[:, :], in_=vb_row).then_inc(S("dCAST"), 16)
                    g.dma_start(out=obr_h[:, :], in_=ob_row).then_inc(S("dCAST"), 16)
                    g.dma_start(out=identb[:, :], in_=ident_in).then_inc(S("dCAST"), 16)

                @blk.vector
                def _(v):
                    for tc in range(TC):
                        v.wait_ge(S("dX1%d" % (tc % 2)), 16 * (tc // 2 + 1))
                        v.tensor_reduce(sums[:, tc:tc + 1], xb[tc % 2][:, :],
                                        axis=AX.X, op=ALU.add
                                        ).then_inc(S("vSUM"), 1)
                    v.wait_ge(S("dLD"), 64)
                    v.wait_ge(S("dCAST"), 64)
                    nbl = 0
                    for b_f, b_h, b_l in ((kbr, kbr_h, kbr_l),
                                          (vbr, vbr_h, vbr_l),
                                          (obr, obr_h, obr_l)):
                        v.tensor_tensor(brow_f[:, :], b_f[:, :], b_h[:, :],
                                        op=ALU.subtract).then_inc(S("vBL"), 1)
                        nbl += 1
                        v.wait_ge(S("vBL"), nbl)
                        v.tensor_copy(b_l[:, :], brow_f[:, :]).then_inc(S("vBL"), 1)
                        nbl += 1
                        v.wait_ge(S("vBL"), nbl)
                    v.wait_ge(S("vSUM"), 8)
                    v.wait_ge(S("aSQ"), 8)
                    n = [0]

                    def stp(instr):
                        instr.then_inc(S("vST"), 1)
                        n[0] += 1
                        v.wait_ge(S("vST"), n[0])
                    stp(v.tensor_scalar(mean8[:, :], sums[:, :], 1.0 / D, None,
                                        op0=ALU.mult))
                    stp(v.tensor_tensor(var8[:, :], mean8[:, :], mean8[:, :],
                                        op=ALU.mult))
                    stp(v.tensor_scalar(sumsq[:, :], sumsq[:, :], 1.0 / D,
                                        None, op0=ALU.mult))
                    stp(v.tensor_tensor(var8[:, :], sumsq[:, :], var8[:, :],
                                        op=ALU.subtract))
                    v.wait_ge(S("aRSD"), 1)
                    stp(v.reciprocal(rstd8[:, :], var8[:, :]))
                    stp(v.tensor_tensor(nmr8[:, :], mean8[:, :], rstd8[:, :],
                                        op=ALU.mult))
                    stp(v.tensor_scalar(nmr8[:, :], nmr8[:, :], -1.0, None,
                                        op0=ALU.mult))

                @blk.scalar
                def _(s):
                    for tc in range(TC):
                        s.wait_ge(S("dX1%d" % (tc % 2)), 16 * (tc // 2 + 1))
                        if tc >= 1:
                            s.wait_ge(S("aSQ"), tc)
                        s.activation(sq_d[:, :], xb[tc % 2][:, :], AF.Square,
                                     accum_out=sumsq[:, tc:tc + 1]
                                     ).then_inc(S("aSQ"), 1)
                    s.wait_ge(S("vST"), 4)
                    s.activation(var8[:, :], var8[:, :], AF.Sqrt,
                                 bias=eps_ln[:, :]).then_inc(S("aRSD"), 1)
                    s.wait_ge(S("vST"), 7)
                    for tc in range(TC):
                        s.wait_ge(S("dX2%d" % (tc % 2)), 16 * (tc // 2 + 1))
                        if tc >= 2:
                            s.wait_ge(S("dXNs%d" % (tc % 2)), 16 * (tc // 2))
                        s.activation(xnb[tc % 2][:, :], xb2[tc % 2][:, :],
                                     AF.Identity, bias=nmr8[:, tc:tc + 1],
                                     scale=rstd8[:, tc:tc + 1]
                                     ).then_inc(S("aXN"), 1)

        with ExitStack() as stP0b:
            tr_att = sb("tr_att", [128, 16, 1024], BF16, stP0b)

            with nc.Block() as blk:
                @blk.sync
                def _(sync):
                    sync.dma_start_transpose(out=xn_fm[:, :, :],
                                             in_=xn_nat.ap()
                                             ).then_inc(S("dTR"), 16)
                    sync.wait_ge(S("dTR"), 16)
                    sync.dma_start_transpose(out=tr_att[:, :, :],
                                             in_=attw_nat.ap()
                                             ).then_inc(S("dTR"), 16)
                    sync.wait_ge(S("dTR"), 32)
                    sync.dma_start(
                        out=sh_att.ap().rearrange("p (c f) -> p c f", f=1024),
                        in_=tr_att[:, :, :]).then_inc(S("dGA"), 16)
                    sync.wait_ge(S("dGA"), 16)

                @blk.gpsimd
                def _(g):
                    g.wait_ge(S("dGA"), 16)
                    g.collective_compute(
                        "AllGather", ALU.bypass,
                        replica_groups=[list(range(8))],
                        ins=[sh_att.ap().opt()], outs=[g_att.ap().opt()]
                    ).then_inc(S("sAF"), 1)

        with ExitStack() as stP0c:
            wb = [sb("wb0", [128, D], BF16, stP0c),
                  sb("wb1", [128, D], BF16, stP0c)]
            qt_gr = [sb("qt_gr%d" % k, [128, 32], F32, stP0c) for k in range(2)]
            qt_s = [sb("qt_s%d" % k, [128, 32], BF16, stP0c) for k in range(2)]
            qt_t = [sb("qt_t%d" % k, [128, 32], F32, stP0c) for k in range(2)]
            qt_nt = [sb("qt_nt%d" % k, [128, 32], F32, stP0c) for k in range(2)]
            qt_c = [sb("qt_c%d" % k, [128, D], BF16, stP0c) for k in range(2)]
            qt_u = [sb("qt_u%d" % k, [128, D], BF16, stP0c) for k in range(2)]
            qt_q = [sb("qt_q%d" % k, [128, D], BF16, stP0c) for k in range(2)]
            qt_w = [sb("qt_w%d" % k, [128, D], BF16, stP0c) for k in range(2)]
            qt_r = [sb("qt_r%d" % k, [128, D], BF16, stP0c) for k in range(2)]
            stg = [sb("stg%d" % k, [128, 16, 512], BF16, stP0c) for k in range(2)]
            wf = [sb("wf%d" % k, [128, D], F32, stP0c) for k in range(2)]
            ptp = [pst("ptp%d" % k, [128, 512], stP0c, dt=BF16) for k in range(2)]

            NT = 24

            def vq_after_B(i):
                return 6 * min(i + 2, NT) + 3 * (i + 1)

            def tile_A(v, i):
                k = i % 2
                buf = wb[k]
                v.wait_ge(S("aWB"), i + 1)
                def stq(instr):
                    instr.then_inc(S("vQ"), 1)
                    tile_A.n += 1
                    v.wait_ge(S("vQ"), tile_A.n)
                stq(v.tensor_reduce(
                    qt_gr[k][:, :],
                    buf[:, :].rearrange("p (g c) -> p g c", c=64),
                    axis=AX.X, op=ALU.add, apply_absolute_value=True))
                stq(v.tensor_scalar(qt_s[k][:, :], qt_gr[k][:, :], 1.0 / 64,
                                    1e-8, op0=ALU.mult, op1=ALU.max))
                stq(v.tensor_scalar(qt_t[k][:, :], qt_s[k][:, :], 0.5, None,
                                    op0=ALU.mult))
                stq(v.tensor_scalar(qt_nt[k][:, :], qt_s[k][:, :], -0.5,
                                    None, op0=ALU.mult))
                stq(v.tensor_tensor(
                    qt_c[k][:, :].rearrange("p (g c) -> p g c", c=64),
                    buf[:, :].rearrange("p (g c) -> p g c", c=64),
                    qt_t[k][:, :].unsqueeze(2).broadcast_to([128, 32, 64]),
                    op=ALU.min))
                stq(v.tensor_tensor(
                    qt_c[k][:, :].rearrange("p (g c) -> p g c", c=64),
                    qt_c[k][:, :].rearrange("p (g c) -> p g c", c=64),
                    qt_nt[k][:, :].unsqueeze(2).broadcast_to([128, 32, 64]),
                    op=ALU.max))
                v.tensor_tensor(qt_u[k][:, :], buf[:, :], qt_c[k][:, :],
                                op=ALU.subtract).then_inc(S("vQu"), 1)
            tile_A.n = 0

            def tile_B(v, i):
                k = i % 2
                buf = wb[k]
                v.wait_ge(S("aSG"), i + 1)
                def stq(instr):
                    instr.then_inc(S("vQ"), 1)
                    tile_A.n += 1
                    v.wait_ge(S("vQ"), tile_A.n)
                stq(v.tensor_tensor(
                    qt_w[k][:, :].rearrange("p (g c) -> p g c", c=64),
                    qt_q[k][:, :].rearrange("p (g c) -> p g c", c=64),
                    qt_s[k][:, :].unsqueeze(2).broadcast_to([128, 32, 64]),
                    op=ALU.mult))
                stq(v.tensor_tensor(qt_c[k][:, :], qt_w[k][:, :], buf[:, :],
                                    op=ALU.subtract))
                if i >= 2:
                    v.wait_ge(S("peTP"), 4 * (i - 1))
                stq(v.tensor_tensor(qt_r[k][:, :], buf[:, :], qt_c[k][:, :],
                                    op=ALU.add))

            with nc.Block(no_gpsimd_drain=True) as blk:
                @blk.sync
                def _(sync):
                    for i in range(NT):
                        if i >= 2:
                            sync.wait_ge(S("aWB"), i - 1)
                        if i < 16:
                            sync.dma_start(
                                out=wf[i % 2][:, :],
                                in_=wgu_in[i * 128:(i + 1) * 128, :]
                            ).then_inc(S("dW%d" % (i % 2)), 16)
                        else:
                            j = i - 16
                            sync.dma_start(
                                out=wf[i % 2][:, :].rearrange(
                                    "p (c f) -> p c f", f=128),
                                in_=wpj_in[:, j * 128:(j + 1) * 128].rearrange(
                                    "(c p) f -> p c f", p=128)
                            ).then_inc(S("dW%d" % (i % 2)), 16)

                @blk.gpsimd
                def _(g):
                    g.wait_ge(S("dSG0"), 32)
                    g.wait_ge(S("dSG1"), 32)
                    g.collective_compute(
                        "AllGather", ALU.bypass,
                        replica_groups=[list(range(8))],
                        ins=[sh_wgu.ap().opt()], outs=[g_wgu.ap().opt()]
                    ).then_inc(S("sFF"), 1)
                    g.wait_ge(S("dSG0"), 16 * 6)
                    g.wait_ge(S("dSG1"), 16 * 6)
                    g.collective_compute(
                        "AllGather", ALU.bypass,
                        replica_groups=[list(range(8))],
                        ins=[sh_wpj.ap().opt()], outs=[g_wpj.ap().opt()]
                    ).then_inc(S("sFF"), 1)

                @blk.vector
                def _(v):
                    for i in range(NT):
                        tile_A(v, i)
                        if i >= 1:
                            tile_B(v, i - 1)
                    tile_B(v, NT - 1)

                shu = sh_wgu.ap().rearrange("p (c f) -> p c f", f=2048)
                shp = sh_wpj.ap().rearrange("p (c f) -> p c f", f=2048)

                def _evacs(s, i):
                    for b in range(4):
                        tb = 4 * i + b
                        s.wait_ge(S("peTP"), tb + 1)
                        if i < 16:
                            gidx = i // 4
                            if b == 0 and i % 4 == 0 and gidx >= 2:
                                s.wait_ge(S("dSG%d" % (gidx % 2)),
                                          16 * (gidx // 2))
                            dst = stg[gidx % 2][:, 4 * b:4 * b + 4,
                                                (i % 4) * 128:(i % 4) * 128 + 128]
                        else:
                            j = i - 16
                            if b == 0:
                                s.wait_ge(S("dSG%d" % (j % 2)),
                                          16 * (2 + j // 2))
                            dst = stg[j % 2][:, 4 * b:4 * b + 4, 0:128]
                        s.activation(dst,
                                     ptp[tb % 2][:, :].rearrange(
                                         "p (c f) -> p c f", f=128),
                                     AF.Identity).then_inc(S("aTW"), 1)
                    if i < 16 and i % 4 == 3:
                        gidx = i // 4
                        s.wait_ge(S("aTW"), 16 * (gidx + 1))
                        s.dma_start(
                            out=shu[:, :, gidx * 512:(gidx + 1) * 512],
                            in_=stg[gidx % 2][:, :, :]
                        ).then_inc(S("dSG%d" % (gidx % 2)), 16)
                    elif i >= 16:
                        j = i - 16
                        s.wait_ge(S("aTW"), 4 * (i + 1))
                        s.dma_start(
                            out=shp[:, j, :].rearrange("p (c f) -> p c f",
                                                       f=128),
                            in_=stg[j % 2][:, :, 0:128]
                        ).then_inc(S("dSG%d" % (j % 2)), 16)

                @blk.scalar
                def _(s):
                    for i in range(NT + 1):
                        if i >= 1:
                            s.wait_ge(S("vQu"), i)
                            s.activation(qt_q[(i - 1) % 2][:, :],
                                         qt_u[(i - 1) % 2][:, :],
                                         AF.Sign).then_inc(S("aSG"), 1)
                        if i < NT:
                            s.wait_ge(S("dW%d" % (i % 2)), 16 * (i // 2 + 1))
                            if i >= 2:
                                s.wait_ge(S("vQ"), vq_after_B(i - 2))
                            s.activation(wb[i % 2][:, :], wf[i % 2][:, :],
                                         AF.Identity).then_inc(S("aWB"), 1)
                        if i >= 2:
                            _evacs(s, i - 2)
                    _evacs(s, NT - 1)

                @blk.tensor
                def _(t):
                    for i in range(NT):
                        t.wait_ge(S("vQ"), vq_after_B(i))
                        for b in range(4):
                            tb = 4 * i + b
                            if tb >= 2:
                                t.wait_ge(S("aTW"), tb - 1)
                            for c4 in range(4):
                                dc = b * 4 + c4
                                mm = t.transpose(
                                    ptp[tb % 2][:, c4 * 128:(c4 + 1) * 128],
                                    qt_r[i % 2][:, dc * 128:(dc + 1) * 128],
                                    identb[:, :])
                                if c4 == 3:
                                    mm.then_inc(S("peTP"), 1)


        if PHASES <= 2:
            return nc

        stP4out = st0.enter_context(ExitStack())
        qw = [sb("qw0", [128, 16, 512], BF16, stP4out),
              sb("qw1", [128, 16, 512], BF16, stP4out)]
        with ExitStack() as stP4a:
            kp_tm = sb("kp_tm", [128, TC, D], BF16, stP4a)
            v_ext = sb("v_ext", [128, TC, HEADS, 132], BF16, stP4a)
            wwin = [sb("wwin0", [128, 16, 512], BF16, stP4a),
                    sb("wwin1", [128, 16, 512], BF16, stP4a)]
            scrm = [sb("scrm0", [128, 512], F32, stP4a)] * 2
            scre = [sb("scre0", [128, 512], F32, stP4a)] * 2
            scrr = [sb("scrr0", [128, 512], F32, stP4a)] * 2
            pk = [pst("pk0", [128, 512], stP4a), pst("pk1", [128, 512], stP4a)]
            pkv = [pst("pkv%d" % i, [128, 396], stP4a) for i in range(6)]

            def kv_slot(h):
                return pkv[h // 3][:, (h % 3) * 132:(h % 3) * 132 + 129]

            with nc.Block(no_gpsimd_drain=True) as blk:
                @blk.gpsimd
                def _(g):
                    g.memset(v_ext[:, :, :, 128:129], 1.0).then_inc(S("vONE"), 1)
                    g.wait_ge(S("sAF"), 1)
                    for wi in range(4, 8):
                        w = wi % 4
                        rank = 4 + w // 2
                        off = (w % 2) * 512
                        g.wait_ge(S("pePJ"), (wi - 1) * TC)
                        g.dma_start(
                            out=wwin[wi % 2][:, :, :],
                            in_=gv_att[rank, :, :, off:off + 512]
                        ).then_inc(S("dVW%d" % (wi % 2)), 16)

                @blk.sync
                def _(sync):
                    sync.wait_ge(S("sAF"), 1)
                    for wi in range(4):
                        rank = 2 + wi // 2
                        off = (wi % 2) * 512
                        if wi >= 2:
                            sync.wait_ge(S("pePJ"), (wi - 1) * TC)
                        sync.dma_start(
                            out=wwin[wi % 2][:, :, :],
                            in_=gv_att[rank, :, :, off:off + 512]
                        ).then_inc(S("dKW%d" % (wi % 2)), 16)
                    for j in range(2):
                        sync.dma_start(
                            out=qw[j][:, :, :],
                            in_=gv_att[0, :, :, j * 512:j * 512 + 512]
                        ).then_inc(S("dQW%d" % (j % 2)), 16)

                @blk.tensor
                def _(t):
                    for wi in range(8):
                        is_k = wi < 4
                        w = wi % 4
                        buf = wwin[wi % 2]
                        bh, bl = (kbr_h, kbr_l) if is_k else (vbr_h, vbr_l)
                        if is_k:
                            t.wait_ge(S("dKW%d" % (wi % 2)), 16 * (wi // 2 + 1))
                        else:
                            t.wait_ge(S("dVW%d" % (wi % 2)),
                                      16 * ((wi - 4) // 2 + 1))
                        for tc in range(TC):
                            gi = wi * TC + tc
                            pb = pk[gi % 2]
                            if gi >= 2:
                                t.wait_ge(S("vKP"), gi - 1)
                            t.matmul(pb[:, :], ones_r[0:1, 0:128],
                                     bh[0:1, w * 512:(w + 1) * 512],
                                     start=True, stop=False)
                            t.matmul(pb[:, :], ones_r[0:1, 0:128],
                                     bl[0:1, w * 512:(w + 1) * 512],
                                     start=False, stop=False)
                            for dc in range(NDC):
                                mm = t.matmul(pb[:, :],
                                              xn_fm[:, dc, tc * 128:(tc + 1) * 128],
                                              buf[:, dc, :], start=False,
                                              stop=(dc == NDC - 1))
                                if dc == NDC - 1:
                                    mm.then_inc(S("pePJ"), 1)
                    t.wait_ge(S("vKP"), 64)
                    t.wait_ge(S("vONE"), 1)
                    for tc in range(TC):
                        for h in range(HEADS):
                            mm = t.matmul(kv_slot(h),
                                          kp_tm[:, tc, h * 128:(h + 1) * 128],
                                          v_ext[:, tc, h, 0:129],
                                          start=(tc == 0 and h % 3 == 0),
                                          stop=(tc == TC - 1),
                                          skip_group_check=True)
                            if tc == TC - 1 and h == HEADS - 1:
                                mm.then_inc(S("peKVD"), 1)

                @blk.vector
                def _(v):
                    for gi in range(64):
                        wi, tc = divmod(gi, TC)
                        is_k = wi < 4
                        w = wi % 4
                        pb = pk[gi % 2]
                        par = gi % 2
                        v.wait_ge(S("pePJ"), gi + 1)
                        if is_k:
                            v.tensor_scalar(scrm[par][:, :], pb[:, :], 0.0,
                                            None, op0=ALU.min
                                            ).then_inc(S("vMIN"), 1)
                            v.wait_ge(S("aER"), 2 * (gi + 1))
                            v.tensor_tensor(
                                kp_tm[:, tc, w * 512:(w + 1) * 512],
                                scre[par][:, :], scrr[par][:, :], op=ALU.add
                            ).then_inc(S("vKP"), 1)
                        else:
                            v.tensor_copy(
                                v_ext[:, tc, 4 * w:4 * w + 4, 0:128],
                                pb[:, :].rearrange("p (h c) -> p h c", c=128)
                            ).then_inc(S("vKP"), 1)
                    v.wait_ge(S("peKVD"), 1)
                    for h in range(HEADS):
                        v.tensor_copy(kv_f[:, h * 132:h * 132 + 129],
                                      kv_slot(h)).then_inc(S("vKVF"), 1)
                    for h in range(HEADS):
                        mm = v.memset(kv_f[:, h * 132 + 129:h * 132 + 132], 0.0)
                        if h == HEADS - 1:
                            mm.then_inc(S("vKVF"), 1)

                @blk.scalar
                def _(s):
                    for gi in range(32):
                        par = gi % 2
                        s.wait_ge(S("vMIN"), gi + 1)
                        s.activation(scre[par][:, :], scrm[par][:, :], AF.Exp
                                     ).then_inc(S("aER"), 1)
                        s.wait_ge(S("pePJ"), gi + 1)
                        s.activation(scrr[par][:, :], pk[gi % 2][:, :], AF.Relu
                                     ).then_inc(S("aER"), 1)

        if PHASES <= 3:
            return nc

        with ExitStack() as stP4b:
            qp_fm = sb("qp_fm", [128, HEADS, T], BF16, stP4b)
            attn_f = qp_fm
            scrm = [sb("scrm0b", [128, 512], F32, stP4b),
                    sb("scrm1b", [128, 512], F32, stP4b)]
            scre = [sb("scre0b", [128, 512], F32, stP4b),
                    sb("scre1b", [128, 512], F32, stP4b)]
            scrr = [sb("scrr0b", [128, 512], F32, stP4b),
                    sb("scrr1b", [128, 512], F32, stP4b)]
            numb = [sb("numb%d" % i, [128, 512], BF16, stP4b)
                    for i in range(4)]
            zs_all = sb("zs_all", [65, 512], F32, stP4b)
            zbf_all = sb("zbf_all", [65, 512], BF16, stP4b)
            owin = [sb("owin0", [128, 16, 512], BF16, stP4b),
                    sb("owin1", [128, 16, 512], BF16, stP4b)]
            x_re = sb("x_re", [128, 8, 512], F32, stP4b)
            pev = [sb("pev0", [128, 512], F32, stP4b),
                   sb("pev1", [128, 512], F32, stP4b)]
            pq = [pst("pq0", [128, 512], stP4b), pst("pq1", [128, 512], stP4b)]
            pn = [pst("pn0", [128, 512], stP4b), pst("pn1", [128, 512], stP4b)]
            pz = [pst("pz0", [1, 512], stP4b), pst("pz1", [1, 512], stP4b)]
            pr = [pst("pr0", [128, 512], stP4b), pst("pr1", [128, 512], stP4b)]

            with nc.Block() as blk:
                @blk.gpsimd
                def _(g):
                    g.memset(zs_all[:, :], 1.0).then_inc(S("gZS"), 1)
                    g.wait_ge(S("vKVF"), 17)
                    g.dma_start(out=ar_i.ap(), in_=kv_f[:, :]).then_inc(S("dAR"), 16)
                    g.wait_ge(S("dAR"), 16)
                    g.collective_compute(
                        "AllReduce", ALU.add,
                        replica_groups=[[0, 1, 2, 3], [4, 5, 6, 7]],
                        ins=[ar_i.ap().opt()], outs=[ar_o.ap().opt()]
                    ).then_inc(S("sAR"), 1)
                    g.wait_ge(S("sAR"), 1)
                    g.dma_start(out=kv_f[:, :], in_=ar_o.ap()).then_inc(S("dAR"), 16)

                @blk.sync
                def _(sync):
                    for j in range(2, 4):
                        rank = j // 2
                        off = (j % 2) * 512
                        sync.wait_ge(S("peQ"), 8 * (j - 1))
                        sync.dma_start(
                            out=qw[j % 2][:, :, :],
                            in_=gv_att[rank, :, :, off:off + 512]
                        ).then_inc(S("dQW%d" % (j % 2)), 16)
                    for w in range(NW):
                        rank = 6 + w // 2
                        off = (w % 2) * 512
                        if w >= 2:
                            sync.wait_ge(S("peO"), (w - 1) * TC)
                        sync.dma_start(
                            out=owin[w % 2][:, :, :],
                            in_=gv_att[rank, :, :, off:off + 512]
                        ).then_inc(S("dOW%d" % (w % 2)), 16)
                        if w >= 1:
                            sync.wait_ge(S("vPART"), w * TC)
                        sync.dma_start(
                            out=x_re[:, :, :],
                            in_=x_in[:, w * 512:(w + 1) * 512].rearrange(
                                "(c p) f -> p c f", p=128)
                        ).then_inc(S("dXR"), 16)
                        for tc in range(TC):
                            gi = w * TC + tc
                            sync.wait_ge(S("vPART"), gi + 1)
                            sync.dma_start(
                                out=part_nat.ap()[tc * 128:(tc + 1) * 128,
                                                  w * 512:(w + 1) * 512],
                                in_=pev[tc % 2][:, :]
                            ).then_inc(S("dPT%d" % (tc % 2)), 16)
                    sync.wait_ge(S("dPT0"), 16 * 16)
                    sync.wait_ge(S("dPT1"), 16 * 16)

                @blk.tensor
                def _(t):
                    for m in range(16):
                        j = m // 4
                        t.wait_ge(S("dQW%d" % (j % 2)), 16 * (j // 2 + 1))
                        for th in range(2):
                            gi = m * 2 + th
                            pb = pq[gi % 2]
                            if gi >= 2:
                                t.wait_ge(S("vQP"), gi - 1)
                            for dc in range(NDC):
                                mm = t.matmul(pb[:, :],
                                              qw[j % 2][:, dc,
                                                        (m % 4) * 128:
                                                        (m % 4) * 128 + 128],
                                              xn_fm[:, dc,
                                                    th * 512:th * 512 + 512],
                                              start=(dc == 0),
                                              stop=(dc == NDC - 1))
                                if dc == NDC - 1:
                                    mm.then_inc(S("peQ"), 1)
                    t.wait_ge(S("vKVB"), 1)
                    for gi in range(35):
                        if gi >= 3:
                            pj = gi - 3
                            t.wait_ge(S("vZB"), pj // 3 + 1)
                            if pj >= 2:
                                t.wait_ge(S("vAT"), pj - 1)
                            zr = (pj % 3) * 32
                            t.matmul(pr[pj % 2][:, :],
                                     ones3[zr:zr + 1, 0:128],
                                     zbf_all[zr:zr + 1, :],
                                     start=True, stop=True
                                     ).then_inc(S("peR"), 1)
                        if gi < 32:
                            h, th = divmod(gi, 2)
                            par = gi % 2
                            if gi >= 2:
                                t.wait_ge(S("vNB"), gi - 1)
                                t.wait_ge(S("vZT"), gi - 1)
                            t.wait_ge(S("vQP"), gi + 1)
                            t.matmul(pn[par][:, :],
                                     kv_b[:, h * 132:h * 132 + 128],
                                     qp_fm[:, h, th * 512:th * 512 + 512],
                                     start=True, stop=True)
                            t.matmul(pz[par][:, :],
                                     kv_b[:, h * 132 + 128:h * 132 + 129],
                                     qp_fm[:, h, th * 512:th * 512 + 512],
                                     start=True, stop=True).then_inc(S("peN"), 1)
                    t.wait_ge(S("vAT"), 32)
                    for w in range(NW):
                        t.wait_ge(S("dOW%d" % (w % 2)), 16 * (w // 2 + 1))
                        for tc in range(TC):
                            gi = w * TC + tc
                            pb = pq[gi % 2]
                            if gi >= 2:
                                t.wait_ge(S("vPART"), gi - 1)
                            t.matmul(pb[:, :], ones_r[0:1, 0:128],
                                     obr_h[0:1, w * 512:(w + 1) * 512],
                                     start=True, stop=False)
                            t.matmul(pb[:, :], ones_r[0:1, 0:128],
                                     obr_l[0:1, w * 512:(w + 1) * 512],
                                     start=False, stop=False)
                            for fc in range(NDC):
                                mm = t.matmul(pb[:, :],
                                              attn_f[:, fc,
                                                     tc * 128:tc * 128 + 128],
                                              owin[w % 2][:, fc, :],
                                              start=False,
                                              stop=(fc == NDC - 1))
                                if fc == NDC - 1:
                                    mm.then_inc(S("peO"), 1)

                @blk.vector
                def _(v):
                    for gi in range(32):
                        m, th = divmod(gi, 2)
                        par = gi % 2
                        v.wait_ge(S("peQ"), gi + 1)
                        v.tensor_scalar(scrm[par][:, :], pq[par][:, :],
                                        qbc[:, m:m + 1], 0.0, op0=ALU.add,
                                        op1=ALU.min).then_inc(S("vQM"), 1)
                        v.wait_ge(S("aQE"), 2 * (gi + 1))
                        v.tensor_tensor(qp_fm[:, m, th * 512:th * 512 + 512],
                                        scre[par][:, :], scrr[par][:, :],
                                        op=ALU.add).then_inc(S("vQP"), 1)
                    v.wait_ge(S("dAR"), 32)
                    v.tensor_copy(kv_b[:, :], kv_f[:, :]).then_inc(S("vKVB"), 1)
                    v.wait_ge(S("gZS"), 1)
                    for gi in range(32):
                        par = gi % 2
                        v.wait_ge(S("peN"), gi + 1)
                        zr = (gi % 3) * 32
                        v.tensor_scalar(zs_all[zr:zr + 1, :], pz[par][:, :],
                                        ATT_EPS, None, op0=ALU.add
                                        ).then_inc(S("vZT"), 1)
                        v.tensor_copy(numb[gi % 4][:, :], pn[par][:, :]
                                      ).then_inc(S("vNB"), 1)
                        if gi % 3 == 2 or gi == 31:
                            g3 = gi // 3
                            v.wait_ge(S("vZT"), gi + 1)
                            v.reciprocal(zs_all[:, :], zs_all[:, :]
                                         ).then_inc(S("vZ2"), 1)
                            v.wait_ge(S("vZ2"), g3 + 1)
                            if g3 >= 1:
                                v.wait_ge(S("peR"), g3 * 3)
                            v.tensor_copy(zbf_all[:, :], zs_all[:, :]
                                          ).then_inc(S("vZB"), 1)
                            for pj in range(g3 * 3, gi + 1):
                                hj, thj = divmod(pj, 2)
                                v.wait_ge(S("peR"), pj + 1)
                                v.tensor_tensor(
                                    attn_f[:, hj, thj * 512:thj * 512 + 512],
                                    numb[pj % 4][:, :], pr[pj % 2][:, :],
                                    op=ALU.mult).then_inc(S("vAT"), 1)
                    for w in range(NW):
                        v.wait_ge(S("dXR"), 16 * (w + 1))
                        for tc in range(TC):
                            gi = w * TC + tc
                            v.wait_ge(S("peO"), gi + 1)
                            if gi >= 2:
                                v.wait_ge(S("dPT%d" % (tc % 2)), 16 * (gi // 2))
                            v.tensor_tensor(pev[tc % 2][:, :], pq[gi % 2][:, :],
                                            x_re[:, tc, :], op=ALU.add
                                            ).then_inc(S("vPART"), 1)

                @blk.scalar
                def _(s):
                    for gi in range(32):
                        m = gi // 2
                        par = gi % 2
                        s.wait_ge(S("vQM"), gi + 1)
                        s.activation(scre[par][:, :], scrm[par][:, :], AF.Exp
                                     ).then_inc(S("aQE"), 1)
                        s.wait_ge(S("peQ"), gi + 1)
                        s.activation(scrr[par][:, :], pq[par][:, :], AF.Relu,
                                     bias=qbc[:, m:m + 1]).then_inc(S("aQE"), 1)

        if PHASES <= 4:
            return nc
        stP4out.close()
        stB.close()

        with ExitStack() as stP5:
            h_fm = sb("h_fm", [128, NHC, T], BF16, stP5)
            ones_cb = sb("ones_cb", [128, 1], BF16, stP5)
            s8 = sb("s8", [128, 8], F32, stP5)

            with ExitStack() as stP5a:
                gw = [sb("gw0", [128, 16, 256], BF16, stP5a),
                      sb("gw1", [128, 16, 256], BF16, stP5a)]
                uw = [sb("uw0", [128, 16, 256], BF16, stP5a),
                      sb("uw1", [128, 16, 256], BF16, stP5a)]
                silu_sb = sb("silu_sb", [128, 1024], BF16, stP5a)
                sqs = [sb("sqs0", [128, 1024], BF16, stP5a),
                       sb("sqs1", [128, 1024], BF16, stP5a)]
                pga = [pst("pga0", [128, 512], stP5a),
                       pst("pga1", [128, 512], stP5a)]
                pua = [pst("pua0", [128, 512], stP5a),
                       pst("pua1", [128, 512], stP5a)]
                pS = pst("pS", [128, 512], stP5a)

                with nc.Block() as blk:
                    @blk.gpsimd
                    def _(g):
                        g.memset(ones_cb[:, :], 1.0).then_inc(S("vONE2"), 1)
                        g.wait_ge(S("sFF"), 1)
                        for bidx in range(32):
                            rank_g = bidx // 8
                            off = (bidx % 8) * 256
                            if bidx >= 2:
                                g.wait_ge(S("peU"), 2 * (bidx - 1))
                            g.dma_start(
                                out=uw[bidx % 2][:, :, :],
                                in_=gv_wgu[rank_g + 4, :, :, off:off + 256]
                            ).then_inc(S("dUW%d" % (bidx % 2)), 16)

                    @blk.sync
                    def _(sync):
                        sync.wait_ge(S("sFF"), 1)
                        for bidx in range(32):
                            rank_g = bidx // 8
                            off = (bidx % 8) * 256
                            if bidx >= 2:
                                sync.wait_ge(S("peU"), 2 * (bidx - 1))
                            sync.dma_start(
                                out=gw[bidx % 2][:, :, :],
                                in_=gv_wgu[rank_g, :, :, off:off + 256]
                            ).then_inc(S("dGW%d" % (bidx % 2)), 16)

                    @blk.tensor
                    def _(t):
                        t.wait_ge(S("vONE2"), 1)
                        for hc in range(NHC):
                            bidx = hc // 2
                            j = hc % 2
                            t.wait_ge(S("dGW%d" % (bidx % 2)),
                                      16 * (bidx // 2 + 1))
                            t.wait_ge(S("dUW%d" % (bidx % 2)),
                                      16 * (bidx // 2 + 1))
                            if hc >= 1:
                                t.wait_ge(S("aSL"), 2 * hc)
                            for th in range(2):
                                for dc in range(NDC):
                                    mm = t.matmul(pga[th][:, :],
                                                  gw[bidx % 2][:, dc,
                                                               j * 128:j * 128 + 128],
                                                  xn_fm[:, dc,
                                                        th * 512:th * 512 + 512],
                                                  start=(dc == 0),
                                                  stop=(dc == NDC - 1))
                                    if th == 1 and dc == NDC - 1:
                                        mm.then_inc(S("peG"), 1)
                            if hc >= 1:
                                t.wait_ge(S("vH"), 2 * hc)
                            for th in range(2):
                                for dc in range(NDC):
                                    mm = t.matmul(pua[th][:, :],
                                                  uw[bidx % 2][:, dc,
                                                               j * 128:j * 128 + 128],
                                                  xn_fm[:, dc,
                                                        th * 512:th * 512 + 512],
                                                  start=(dc == 0),
                                                  stop=(dc == NDC - 1))
                                    if th == 1 and dc == NDC - 1:
                                        mm.then_inc(S("peU"), 1)
                            if hc >= 1:
                                t.wait_ge(S("aSQ5"), hc)
                                sq = sqs[(hc - 1) % 2]
                                for tc in range(TC):
                                    mm = t.matmul(pS[:, (hc - 1) + tc * 64:
                                                     (hc - 1) + tc * 64 + 1],
                                                  sq[:, tc * 128:tc * 128 + 128],
                                                  ones_cb[:, :],
                                                  start=(hc - 1 == 0 and tc == 0),
                                                  stop=False,
                                                  skip_group_check=True)
                                    if tc == TC - 1:
                                        mm.then_inc(S("peS"), 1)
                        t.wait_ge(S("aSQ5"), NHC)
                        sq = sqs[(NHC - 1) % 2]
                        for tc in range(TC):
                            mm = t.matmul(pS[:, (NHC - 1) + tc * 64:
                                             (NHC - 1) + tc * 64 + 1],
                                          sq[:, tc * 128:tc * 128 + 128],
                                          ones_cb[:, :], start=False,
                                          stop=(tc == TC - 1),
                                          skip_group_check=True)
                            if tc == TC - 1:
                                mm.then_inc(S("peS"), 1)

                    @blk.scalar
                    def _(s):
                        for hc in range(NHC):
                            s.wait_ge(S("peG"), hc + 1)
                            for th in range(2):
                                s.activation(
                                    silu_sb[:, th * 512:th * 512 + 512],
                                    pga[th][:, :], AF.Silu).then_inc(S("aSL"), 1)
                            s.wait_ge(S("vH"), 2 * (hc + 1))
                            if hc >= 2:
                                s.wait_ge(S("peS"), hc - 1)
                            s.activation(sqs[hc % 2][:, :], h_fm[:, hc, :],
                                         AF.Square).then_inc(S("aSQ5"), 1)
                        s.wait_ge(S("vRS"), 1)
                        s.activation(s8[:, :], s8[:, :], AF.Sqrt,
                                     scale=1.0 / 8192.0, bias=eps_rms[:, :]
                                     ).then_inc(S("aRS"), 1)

                    @blk.vector
                    def _(v):
                        for hc in range(NHC):
                            v.wait_ge(S("peU"), hc + 1)
                            v.wait_ge(S("aSL"), 2 * (hc + 1))
                            for th in range(2):
                                v.tensor_tensor(
                                    h_fm[:, hc, th * 512:th * 512 + 512],
                                    silu_sb[:, th * 512:th * 512 + 512],
                                    pua[th][:, :], op=ALU.mult
                                ).then_inc(S("vH"), 1)
                        v.wait_ge(S("peS"), NHC)
                        v.tensor_reduce(
                            s8[:, :],
                            pS[:, :].rearrange("p (t h) -> p t h", h=64),
                            axis=AX.X, op=ALU.add).then_inc(S("vRS"), 1)
                        v.wait_ge(S("aRS"), 1)
                        v.reciprocal(rstd_f[:, :], s8[:, :]).then_inc(S("vRS"), 1)

            with ExitStack() as stP5c:
                pw = [sb("pw%d" % i, [128, 4, 512], BF16, stP5c)
                      for i in range(4)]
                part_re = sb("part_re", [128, 8, 512], F32, stP5c)
                ev1 = [sb("ev1%d" % i, [128, 512], F32, stP5c)
                       for i in range(4)]
                pp = [pst("pp%d" % i, [128, 512], stP5c) for i in range(8)]

                with nc.Block() as blk:
                    @blk.sync
                    def _(sync):
                        sync.wait_ge(S("sFF"), 2)
                        sync.wait_ge(S("dPT0"), 16 * 16)
                        sync.wait_ge(S("dPT1"), 16 * 16)
                        for w in range(NW):
                            if w >= 1:
                                sync.wait_ge(S("vE1"), w * TC)
                            sync.dma_start(
                                out=part_re[:, :, :],
                                in_=part_nat.ap()[:, w * 512:(w + 1) * 512]
                                .rearrange("(c p) f -> p c f", p=128)
                            ).then_inc(S("dPR"), 16)
                            for blkk in range(0, 16, 2):
                                gbi = w * 16 + blkk
                                rb = blkk // 2
                                if gbi >= 4:
                                    sync.wait_ge(S("pePB"), gbi - 3)
                                sync.dma_start(
                                    out=pw[gbi % 4][:, :, :],
                                    in_=gv_wpj[rb, :, 0:4,
                                               w * 512:(w + 1) * 512]
                                ).then_inc(S("dPW%d" % (gbi % 4)), 16)
                            for tc in range(TC):
                                gi = w * TC + tc
                                sync.wait_ge(S("vE1"), gi + 1)
                                sync.dma_start(
                                    out=out_d[tc * 128:(tc + 1) * 128,
                                              w * 512:(w + 1) * 512],
                                    in_=ev1[tc % 4][:, :]
                                ).then_inc(S("dOUT%d" % (tc % 4)), 16)
                        for k in range(4):
                            sync.wait_ge(S("dOUT%d" % k), 16 * 8)

                    @blk.gpsimd
                    def _(g):
                        g.wait_ge(S("sFF"), 2)
                        for w in range(NW):
                            for blkk in range(1, 16, 2):
                                gbi = w * 16 + blkk
                                rb = blkk // 2
                                if gbi >= 4:
                                    g.wait_ge(S("pePB"), gbi - 3)
                                g.dma_start(
                                    out=pw[gbi % 4][:, :, :],
                                    in_=gv_wpj[rb, :, 4:8,
                                               w * 512:(w + 1) * 512]
                                ).then_inc(S("dPW%d" % (gbi % 4)), 16)

                    @blk.tensor
                    def _(t):
                        for w in range(NW):
                            for hc in range(NHC):
                                gbi = w * 16 + hc // 4
                                t.wait_ge(S("dPW%d" % (gbi % 4)),
                                          16 * (gbi // 4 + 1))
                                for tc in range(TC):
                                    if w >= 1 and hc == 0:
                                        t.wait_ge(S("vE1"), w * TC + tc + 1 - TC)
                                    mm = t.matmul(pp[tc][:, :],
                                                  h_fm[:, hc,
                                                       tc * 128:tc * 128 + 128],
                                                  pw[gbi % 4][:, hc % 4, :],
                                                  start=(hc == 0),
                                                  stop=(hc == NHC - 1))
                                    if hc == NHC - 1 and tc != TC - 1:
                                        mm.then_inc(S("peTCD"), 1)
                                    if hc % 4 == 3 and tc == TC - 1:
                                        mm.then_inc(S("pePB"), 1)

                    @blk.vector
                    def _(v):
                        v.wait_ge(S("vRS"), 2)
                        for w in range(NW):
                            v.wait_ge(S("dPR"), 16 * (w + 1))
                            for tc in range(TC):
                                gi = w * TC + tc
                                if gi >= 4:
                                    v.wait_ge(S("dOUT%d" % (tc % 4)),
                                              16 * (gi // 4))
                                if tc == TC - 1:
                                    v.wait_ge(S("pePB"), 16 * (w + 1))
                                else:
                                    v.wait_ge(S("peTCD"),
                                              w * (TC - 1) + tc + 1)
                                v.scalar_tensor_tensor(
                                    ev1[tc % 4][:, :], pp[tc][:, :],
                                    rstd_f[:, tc:tc + 1], part_re[:, tc, :],
                                    op0=ALU.mult, op1=ALU.add
                                ).then_inc(S("vE1"), 1)
    return nc


_NC_CACHE = {}


def kernel(L0_attn_hidden_states, L0_attn_q_w, L0_attn_q_b, L0_attn_k_w,
           L0_attn_k_b, L0_attn_v_w, L0_attn_v_b, L0_attn_o_w, L0_attn_o_b,
           L0_ffn_w_gate_up, L0_ffn_w_proj):
    x = np.ascontiguousarray(np.asarray(L0_attn_hidden_states, dtype=np.float32))
    ws = [np.ascontiguousarray(np.asarray(w, dtype=np.float32))
          for w in (L0_attn_q_w, L0_attn_k_w, L0_attn_v_w, L0_attn_o_w)]
    wgu = np.ascontiguousarray(np.asarray(L0_ffn_w_gate_up, dtype=np.float32))
    wpj = np.ascontiguousarray(np.asarray(L0_ffn_w_proj, dtype=np.float32))
    qb = np.ascontiguousarray(
        np.asarray(L0_attn_q_b, np.float32).reshape(16, 128).T)
    kb = np.ascontiguousarray(np.asarray(L0_attn_k_b, np.float32).reshape(1, D))
    vb = np.ascontiguousarray(np.asarray(L0_attn_v_b, np.float32).reshape(1, D))
    ob = np.ascontiguousarray(np.asarray(L0_attn_o_b, np.float32).reshape(1, D))

    ident = np.eye(128, dtype=np.float32)
    in_maps = []
    for c in range(8):
        b, qd = c // 4, c % 4
        wi, half = c // 2, c % 2
        in_maps.append({
            "x_slice": np.ascontiguousarray(x[b, qd * T:(qd + 1) * T, :]),
            "attw_slice": np.ascontiguousarray(
                ws[wi][half * 1024:(half + 1) * 1024, :]),
            "wgu_slice": np.ascontiguousarray(wgu[c * 2048:(c + 1) * 2048, :]),
            "wproj_slice": np.ascontiguousarray(wpj[:, c * 1024:(c + 1) * 1024]),
            "qb_col": qb, "kb_row": kb, "vb_row": vb, "ob_row": ob,
            "ident_in": ident,
        })

    if "nc" not in _NC_CACHE:
        _NC_CACHE["nc"] = build()
    res = run_bass_kernel_spmd(_NC_CACHE["nc"], in_maps, core_ids=list(range(8)))
    out = np.empty((2, 4096, D), dtype=np.float32)
    for c in range(8):
        b, qd = c // 4, c % 4
        out[b, qd * T:(qd + 1) * T, :] = np.asarray(res.results[c]["out_slice"])
    return out
